# version 28
# baseline (speedup 1.0000x reference)
import contextlib
import numpy as np

import concourse.bass as bass
import concourse.mybir as mybir
from concourse.bass_utils import run_bass_kernel_spmd

F32 = mybir.dt.float32
BF16 = mybir.dt.bfloat16
AF = mybir.ActivationFunctionType
ALU = mybir.AluOpType
AX = mybir.AxisListType

ENGS = ("pe", "act", "dve", "sync", "gpd")
NDMA = {"sync": 8, "gpd": 4}
SEMS = list(ENGS) + [f"sync{i}" for i in range(8)] + [f"gpd{i}" for i in range(4)]


class Cfg:

    def __init__(self, T=2048, D=1024, F=4096):
        self.T, self.D, self.F = T, D, F
        self.HD = 64
        self.H = D // self.HD
        self.DC = D // 128
        self.KC = T // 128
        self.QW = min(1024, T)
        self.NQW = T // self.QW
        self.NIT = self.H * self.NQW
        self.G = T // 8
        assert self.G % 128 == 0
        self.FC = F // 128
        self.NQ = 4
        self.FQ = F // self.NQ
        self.FCQ = self.FQ // 128
        self.TS = T // 512
        assert self.NIT <= 32


def perm_for(T):
    pid = (np.arange(T) % 64) // 8
    return np.argsort(pid, kind="stable")


class Emitter:

    def __init__(self, sems, active, eng_obj):
        self.sems = sems
        self.active = active
        self.eng = eng_obj
        self.n = {s: 0 for s in SEMS}
        self.ndma = {"sync": 0, "gpd": 0}

    def wait(self, engname, tok):
        if tok is None:
            return
        sem, val = tok
        if val <= 0:
            return
        if engname == self.active:
            self.eng.wait_ge(self.sems[sem], val)

    def op(self, engname, fn):
        self.n[engname] += 1
        tok = (engname, self.n[engname])
        if engname == self.active:
            fn(self.eng).then_inc(self.sems[engname], 1)
        return tok

    def opn(self, engname, fn):
        if engname == self.active:
            fn(self.eng)
        return None

    def dma(self, queue, fn):
        i = self.ndma[queue]
        self.ndma[queue] = i + 1
        k = NDMA[queue]
        sem = f"{queue}{i % k}"
        self.wait(queue, (sem, 16 * (i // k)))
        self.n[sem] += 16
        tok = (sem, self.n[sem])
        if queue == self.active:
            fn(self.eng).then_inc(self.sems[sem], 16)
        return tok


def build_nc(cfg: Cfg):
    nc = bass.Bass("TRN2", target_bir_lowering=False)
    T, D, F, H = cfg.T, cfg.D, cfg.F, cfg.H
    DC, KC, QW = cfg.DC, cfg.KC, cfg.QW

    dr = {}
    dr["xT"] = nc.dram_tensor("xT", [D, T], F32, kind="ExternalInput").ap()
    dr["wqT"] = nc.dram_tensor("wqT", [D, D], F32, kind="ExternalInput").ap()
    dr["wkT"] = nc.dram_tensor("wkT", [D, D], F32, kind="ExternalInput").ap()
    dr["wvT"] = nc.dram_tensor("wvT", [D, D], F32, kind="ExternalInput").ap()
    dr["w1T"] = nc.dram_tensor("w1T", [D, F], F32, kind="ExternalInput").ap()
    dr["w2T"] = nc.dram_tensor("w2T", [F, D], F32, kind="ExternalInput").ap()
    dr["b1r"] = nc.dram_tensor("b1r", [128, F // 128], F32, kind="ExternalInput").ap()
    dr["b2r"] = nc.dram_tensor("b2r", [128, D // 128], F32, kind="ExternalInput").ap()
    dr["ipw"] = nc.dram_tensor("ipw", [1, 1], F32, kind="ExternalInput").ap()
    dr["E2"] = nc.dram_tensor("E2", [2, 128], BF16, kind="ExternalInput").ap()
    dr["outT"] = nc.dram_tensor("outT", [D, T], F32, kind="ExternalOutput").ap()

    ARENA = 206816
    arena = nc.alloc_sbuf_tensor("arena", [128, ARENA], mybir.dt.uint8).ap()
    sb = {}
    ends = []

    def alloc(name, shape, dtype, off):
        esz = 2 if dtype == BF16 else 4
        nbytes = shape[1] * esz
        sb[name] = arena[0:shape[0], off:off + nbytes].bitcast(dtype)
        ends.append(off + nbytes)
        return off + nbytes

    bpr = 2 * DC * T
    alloc("xn", [128, DC * T], BF16, 0)
    oB = bpr
    alloc("sraw", [128, DC * T], BF16, oB)
    alloc("wvb", [128, DC * D], BF16, oB)
    alloc("wkb", [128, DC * D], BF16, oB + 2 * DC * D)
    oC = oB + bpr
    alloc("scrA", [128, 2048], F32, oC)
    alloc("scrB", [128, 2048], F32, oC + 8192)
    oD = oC + 16384
    alloc("wqb", [128, DC * D], BF16, oD + 2 * bpr - 2 * DC * D)
    alloc("xf", [128, DC * T], F32, oD)
    alloc("xbf", [128, DC * T], BF16, oD + 4 * DC * T)
    alloc("xsq", [128, 2 * 2048], BF16, oD + 6 * DC * T)
    alloc("qb", [128, DC * T], BF16, oD)
    alloc("kb", [128, DC * T], BF16, oD + bpr)
    o_expS = oD + 2 * bpr + 2 * KC * H * 65
    o_expS = (o_expS + 31) & ~31
    alloc("vt", [128, KC * (H * 65) + 64], BF16, oD + 2 * bpr)
    alloc("expS", [128, 4 * QW], BF16, o_expS)
    GSL = DC + 1
    o_w1q = oD + 2 * GSL * T
    o_w2q = o_w1q + 2 * DC * cfg.FQ
    o_hq = o_w2q + 2 * cfg.FCQ * D
    o_bt = o_hq + 2 * cfg.FCQ * T
    alloc("gacc", [128, GSL * T], BF16, oD)
    alloc("w1q", [128, DC * cfg.FQ], BF16, o_w1q)
    alloc("w2q", [128, cfg.FCQ * D], BF16, o_w2q)
    alloc("hq", [128, cfg.FCQ * T], BF16, o_hq)
    alloc("btmp", [128, 1024], BF16, o_bt)
    alloc("cbuf", [128, T], F32, o_w1q)
    o_outst = o_w1q + 4 * T if 2 * DC * cfg.FQ >= 8 * T else o_bt + 2048
    alloc("outst", [128, T], F32, o_outst)

    oE = (max(ends) + 31) & ~31
    oE = alloc("b1r", [128, F // 128], F32, oE)
    oE = alloc("b2r", [128, DC], F32, oE)
    oE = alloc("E2", [128, 128], BF16, (oE + 31) & ~31)
    oE = alloc("onesf", [1, 128], F32, (oE + 31) & ~31)
    oE = alloc("ones128f", [128, 1], F32, (oE + 31) & ~31)
    oE = alloc("onesb", [128, 1], BF16, (oE + 31) & ~31)
    oE = alloc("sc_intra", [128, 1], F32, (oE + 31) & ~31)
    oE = alloc("sc_inter", [128, 1], F32, (oE + 31) & ~31)
    oE = alloc("biasm4", [128, 1], F32, (oE + 31) & ~31)
    oE = alloc("epsT", [128, 1], F32, (oE + 31) & ~31)
    oE = alloc("eps2T", [128, 1], F32, (oE + 31) & ~31)
    oE = alloc("thr1", [128, 1], F32, (oE + 31) & ~31)
    oE = alloc("thrn1", [128, 1], F32, (oE + 31) & ~31)
    oE = alloc("thr2", [128, 1], F32, (oE + 31) & ~31)
    oE = alloc("thrn2", [128, 1], F32, (oE + 31) & ~31)
    oE = alloc("ipwt", [1, 8], F32, (oE + 31) & ~31)
    oE = alloc("gred", [128, 32], F32, (oE + 31) & ~31)
    oE = alloc("arena1", [128, 2048], F32, (oE + 31) & ~31)
    alloc("ub0", [128, T], BF16, oE - 8192)
    alloc("ub1", [128, T], BF16, oE - 8192 + 2 * T)
    oE = alloc("Darena", [64, 512], F32, (oE + 31) & ~31)
    oE = alloc("rdenA", [128, 2048], BF16, (oE + 31) & ~31)
    assert oE <= 206816, f"SBUF over budget: {oE}"

    PS = nc.alloc_psum_tensor("PS", [128, 4096], F32).ap()

    es = contextlib.ExitStack()
    sems = {s: es.enter_context(nc.semaphore("s_" + s)) for s in SEMS}

    with nc.Block() as block:
        def run(active):
            def f(eng_obj):
                b = Emitter(sems, active, eng_obj)
                schedule(b, cfg, dr, sb, PS)
            return f
        block.tensor(run("pe"))
        block.scalar(run("act"))
        block.vector(run("dve"))
        block.sync(run("sync"))
        block.gpsimd(run("gpd"))
    es.close()
    return nc


def schedule(b: Emitter, cfg: Cfg, dr, sb, PS):
    T, D, F, H = cfg.T, cfg.D, cfg.F, cfg.H
    DC, KC, QW, NQW, NIT = cfg.DC, cfg.KC, cfg.QW, cfg.NQW, cfg.NIT
    TS, NQ, FCQ, G = cfg.TS, cfg.NQ, cfg.FCQ, cfg.G
    GSL = DC + 1
    xn, sraw = sb["xn"], sb["sraw"]
    PSB = PS.bitcast(BF16)
    scrA, scrB = sb["scrA"], sb["scrB"]
    arena1, Darena, rdenA = sb["arena1"], sb["Darena"], sb["rdenA"]
    nsl = T // 512

    t_E2 = b.dma("sync", lambda s: s.dma_start(out=sb["E2"][0:2, :], in_=dr["E2"][:]))
    t_E2 = b.dma("sync", lambda s: s.dma_start(out=sb["E2"][32:34, :], in_=dr["E2"][:]))
    t_b1r = b.dma("sync", lambda s: s.dma_start(out=sb["b1r"][:], in_=dr["b1r"][:]))
    t_b2r = b.dma("sync", lambda s: s.dma_start(out=sb["b2r"][:], in_=dr["b2r"][:]))
    t_ipw = b.dma("sync", lambda s: s.dma_start(out=sb["ipwt"][0:1, 0:1], in_=dr["ipw"][:]))

    t_onesf = b.op("dve", lambda v: v.memset(sb["onesf"][:], 1.0))
    t_ones128f = b.op("dve", lambda v: v.memset(sb["ones128f"][:], 1.0))
    t_onesb = b.op("dve", lambda v: v.memset(sb["onesb"][:], 1.0))
    t_scintra = b.op("dve", lambda v: v.memset(sb["sc_intra"][:], 0.125))
    t_bias4 = b.op("dve", lambda v: v.memset(sb["biasm4"][:], -4.0))
    t_eps = b.op("dve", lambda v: v.memset(sb["epsT"][:], 1e-5))
    t_eps2 = b.op("dve", lambda v: v.memset(sb["eps2T"][:], 0.5e-5))

    b.wait("act", t_ipw)
    t0 = b.op("act", lambda a: a.activation(out=sb["ipwt"][0:1, 1:2], in_=sb["ipwt"][0:1, 0:1],
                                            func=AF.Exp, scale=-1.0))
    b.wait("dve", t0)
    t1 = b.op("dve", lambda v: v.tensor_single_scalar(out=sb["ipwt"][0:1, 2:3], in_=sb["ipwt"][0:1, 1:2],
                                                      scalar=1.0, op=ALU.add))
    b.wait("dve", t1)
    t2 = b.op("dve", lambda v: v.reciprocal(out=sb["ipwt"][0:1, 3:4], in_=sb["ipwt"][0:1, 2:3]))
    b.wait("pe", t2)
    b.wait("pe", t_onesf)
    t3 = b.op("pe", lambda t: t.matmul(PS[:, 3072:3073], sb["onesf"][0:1, :], sb["ipwt"][0:1, 3:4],
                                       start=True, stop=True))
    b.wait("act", t3)
    t_scint = b.op("act", lambda a: a.activation(out=sb["sc_inter"][:], in_=PS[:, 3072:3073],
                                                 func=AF.Copy, scale=0.125))

    t_xf = [b.dma("sync", lambda s, c=c: s.dma_start(
        out=sb["xf"][:, c * T:(c + 1) * T], in_=dr["xT"][c * 128:(c + 1) * 128, :]))
        for c in range(DC)]

    t_xbf, t_sq = [None] * DC, [None] * DC
    t_sum = [None] * nsl
    t_ssq = [None] * nsl
    t_ssq_chunk = [None] * DC
    for c in range(DC):
        b.wait("dve", t_xf[c])
        t_xbf[c] = b.op("dve", lambda v, c=c: v.tensor_copy(
            out=sb["xbf"][:, c * T:(c + 1) * T], in_=sb["xf"][:, c * T:(c + 1) * T]))
        if c >= 2:
            b.wait("dve", t_ssq_chunk[c - 2])
        b.wait("dve", t_xbf[c])
        t_sq[c] = b.op("dve", lambda v, c=c: v.tensor_tensor(
            out=sb["xsq"][:, (c % 2) * 2048:(c % 2) * 2048 + T],
            in0=sb["xbf"][:, c * T:(c + 1) * T], in1=sb["xbf"][:, c * T:(c + 1) * T], op=ALU.mult))
        b.wait("pe", t_xbf[c])
        b.wait("pe", t_sq[c])
        if c == 0:
            b.wait("pe", t_onesb)
        for s in range(nsl):
            emit = b.op if (c == DC - 1 or s == nsl - 1) else b.opn
            t_sum[s] = emit("pe", lambda t, c=c, s=s: t.matmul(
                PS[0:1, s * 512:(s + 1) * 512], sb["onesb"][:, :],
                sb["xbf"][:, c * T + s * 512:c * T + (s + 1) * 512],
                start=(c == 0), stop=(c == DC - 1), skip_group_check=True))
            t_ssq[s] = emit("pe", lambda t, c=c, s=s: t.matmul(
                PS[32:33, s * 512:(s + 1) * 512], sb["onesb"][:, :],
                sb["xsq"][:, (c % 2) * 2048 + s * 512:(c % 2) * 2048 + (s + 1) * 512],
                start=(c == 0), stop=(c == DC - 1), skip_group_check=True))
        t_ssq_chunk[c] = t_ssq[nsl - 1]

    b.wait("dve", t_ssq_chunk[DC - 1])
    t_vones = b.op("dve", lambda v: v.memset(
        sb["vt"][:, 0:KC * H * 65].rearrange("p (g c) -> p g c", c=65)[:, :, 64:65], 1.0))

    for s in range(nsl):
        if t_sum[s] is not None:
            b.wait("act", t_sum[s])
        if t_ssq[s] is not None:
            b.wait("act", t_ssq[s])
    t_m = b.op("act", lambda a: a.activation(out=arena1[0:1, 0:T], in_=PS[0:1, 0:T],
                                             func=AF.Copy, scale=1.0 / D))
    t_msq = b.op("act", lambda a: a.activation(out=arena1[32:33, 0:T], in_=PS[32:33, 0:T],
                                               func=AF.Copy, scale=1.0 / D))
    b.wait("act", t_m)
    t_m2 = b.op("act", lambda a: a.activation(out=scrA[32:33, 0:T], in_=arena1[0:1, 0:T], func=AF.Square))
    b.wait("dve", t_m2)
    b.wait("dve", t_msq)
    t_var = b.op("dve", lambda v: v.tensor_tensor(out=scrB[32:33, 0:T], in0=arena1[32:33, 0:T],
                                                  in1=scrA[32:33, 0:T], op=ALU.subtract))
    b.wait("act", t_var)
    b.wait("act", t_eps)
    t_lnv = b.op("act", lambda a: a.activation(out=scrA[0:1, 0:T], in_=scrB[32:33, 0:T],
                                               func=AF.Ln, bias=sb["epsT"][0:1, 0:1]))
    b.wait("act", t_lnv)
    t_r = b.op("act", lambda a: a.activation(out=scrB[0:1, 0:T], in_=scrA[0:1, 0:T],
                                             func=AF.Exp, scale=-0.5))
    t_bcm = [None] * nsl
    t_bcr = [None] * nsl
    b.wait("pe", t_m)
    for s in range(nsl):
        t_bcm[s] = b.op("pe", lambda t, s=s: t.matmul(
            PS[:, 2048 + s * 512:2048 + (s + 1) * 512], sb["onesf"][0:1, :],
            arena1[0:1, s * 512:(s + 1) * 512], start=True, stop=True))
    b.wait("pe", t_r)
    for s in range(nsl):
        t_bcr[s] = b.op("pe", lambda t, s=s: t.matmul(
            PS[:, s * 512:(s + 1) * 512], sb["onesf"][0:1, :],
            scrB[0:1, s * 512:(s + 1) * 512], start=True, stop=True))
    pairs = [(c, s) for c in (DC - 2, DC - 1) for s in range(nsl)]
    pairs += [(c, s) for s in range(nsl) for c in range(DC - 2)]
    scr3 = [scrA[:, 0:512], scrB[:, 0:512], arena1[:, 0:512]]
    t_xn_cs = {}
    t_subs = {}
    pend = None
    seen_slices = set()
    for i, (c, s) in enumerate(pairs):
        scol = s * 512
        if s not in seen_slices:
            seen_slices.add(s)
            b.wait("dve", t_bcm[s])
            b.wait("dve", t_bcr[s])
        if i >= 3:
            b.wait("dve", t_xn_cs[pairs[i - 3]])
        t_subs[i] = b.op("dve", lambda v, c=c, scol=scol, i=i: v.tensor_tensor(
            out=scr3[i % 3][:], in0=sb["xf"][:, c * T + scol:c * T + scol + 512],
            in1=PS[:, 2048 + scol:2048 + scol + 512], op=ALU.subtract))
        if pend is not None:
            pc, ps = pairs[pend]
            b.wait("dve", t_subs[pend])
            t_xn_cs[(pc, ps)] = b.op("dve", lambda v, pc=pc, ps=ps, pend=pend: v.tensor_tensor(
                out=xn[:, pc * T + ps * 512:pc * T + ps * 512 + 512],
                in0=scr3[pend % 3][:], in1=PS[:, ps * 512:ps * 512 + 512], op=ALU.mult))
        pend = i
    pc, ps = pairs[pend]
    b.wait("dve", t_subs[pend])
    t_xn_cs[(pc, ps)] = b.op("dve", lambda v, pc=pc, ps=ps, pend=pend: v.tensor_tensor(
        out=xn[:, pc * T + ps * 512:pc * T + ps * 512 + 512],
        in0=scr3[pend % 3][:], in1=PS[:, ps * 512:ps * 512 + 512], op=ALU.mult))
    t_slice_done = [t_xn_cs[(DC - 3, s)] for s in range(nsl)]
    t_xn_last = t_slice_done[nsl - 1]

    def load_weight(wi, dram, dstbuf, extra_wait=None):
        if extra_wait is not None:
            b.wait("gpd", extra_wait)
        return b.dma("gpd", lambda g: g.dma_start(
            out=dstbuf[:], in_=dram.rearrange("(a p) d -> p a d", p=128)))

    t_wv = load_weight(0, dr["wvT"], sb["wvb"])
    t_wk = load_weight(2, dr["wkT"], sb["wkb"], extra_wait=t_xf[DC - 1])
    t_wq = load_weight(1, dr["wqT"], sb["wqb"],
                       extra_wait=t_xn_cs[(DC - 1, nsl - 1)])

    NDS = D // 512
    t_ve = {}
    t_vmm_last = None
    vcol = lambda tc, ds: (tc % 2) * 512 + ds * 2048
    for tc in range(KC):
        if tc >= 2:
            b.wait("pe", t_ve[(tc - 2, NDS - 1)])
            b.wait("pe", t_slice_done[(tc * 128) // 512])
        else:
            b.wait("pe", t_slice_done[min(tc, nsl - 1)])
        b.wait("pe", t_wv)
        for ds in range(NDS):
            tmm = None
            for dinc in range(DC):
                emit = b.op if dinc == DC - 1 else b.opn
                tmm = emit("pe", lambda t, tc=tc, ds=ds, dinc=dinc: t.matmul(
                    PS[:, vcol(tc, ds):vcol(tc, ds) + 512],
                    xn[:, dinc * T + tc * 128:dinc * T + (tc + 1) * 128],
                    sb["wvb"][:, dinc * D + ds * 512:dinc * D + (ds + 1) * 512],
                    start=(dinc == 0), stop=(dinc == DC - 1), skip_group_check=True))
            b.wait("act", tmm)
            b.wait("act", t_vones)
            t_ve[(tc, ds)] = b.op("act", lambda a, tc=tc, ds=ds: a.activation(
                out=sb["vt"][:, tc * (H * 65) + ds * 8 * 65:tc * (H * 65) + (ds * 8 + 8) * 65]
                    .rearrange("p (h c) -> p h c", h=8)[:, :, 0:64],
                in_=PS[:, vcol(tc, ds):vcol(tc, ds) + 512].rearrange("p (h c) -> p h c", h=8),
                func=AF.Copy))
            t_vmm_last = tmm
    t_v_done = t_ve[(KC - 1, NDS - 1)]

    def emit_proj_group(wbuf, t_w, outbuf, dc, base, waits):
        b.wait("pe", t_w)
        for w_ in waits:
            b.wait("pe", w_)
        tmm = None
        for ts in range(TS):
            for dinc in range(DC):
                emit = b.op if (ts == TS - 1 and dinc == DC - 1) else b.opn
                tmm = emit("pe", lambda t, dc=dc, ts=ts, dinc=dinc, base=base: t.matmul(
                    PS[:, base + ts * 512:base + (ts + 1) * 512],
                    wbuf[:, dinc * D + dc * 128:dinc * D + (dc + 1) * 128],
                    xn[:, dinc * T + ts * 512:dinc * T + (ts + 1) * 512],
                    start=(dinc == 0), stop=(dinc == DC - 1), skip_group_check=True))
        b.wait("dve", tmm)
        return b.op("dve", lambda v, dc=dc, base=base: v.tensor_copy(
            out=outbuf[:, dc * T:(dc + 1) * T], in_=PS[:, base:base + T]))

    t_qe = {0: emit_proj_group(sb["wqb"], t_wq, sb["qb"], 0, 0,
                               [t_ve[(KC - 1, NDS - 1)], t_xn_last])}
    t_ke = {0: emit_proj_group(sb["wkb"], t_wk, sb["kb"], 0, 2048,
                               [t_ve[(KC - 1, NDS - 1)], t_xn_last])}

    kcut = DC - (DC * D) // T
    if (DC, cfg.T // 512) == (8, 4):
        weave_plan = {0: [("q", 1)], 1: [("k", 1)], 2: [("q", 2)], 3: [("k", 2)],
                      4: [("q", 3)], 5: [("k", 3)], 6: [("q", 4)], 7: [("q", 5)],
                      8: [("q", 6)], 9: [("q", 7)], 10: [("k", 4)], 11: [("k", 5)],
                      12: [("k", 6)], 13: [("k", 7)]}
        assert kcut == 4
    elif (DC, cfg.T // 512) == (4, 2):
        weave_plan = {0: [("q", 1)], 1: [("k", 1)], 2: [("q", 2), ("q", 3)],
                      3: [("k", 2), ("k", 3)]}
        assert kcut == 2
    else:
        raise AssertionError("no weave plan for this config")
    t_weave_itp = {}

    def emit_weave_half(itp, kind, dc, h, prev_evac):
        wbuf, outbuf, t_w, tdict = ((sb["wqb"], sb["qb"], t_wq, t_qe) if kind == "q"
                                    else (sb["wkb"], sb["kb"], t_wk, t_ke))
        fh = 2048 + (1 - itp % 2) * 1024
        tss = list(range(2 * h, min(2 * h + 2, TS)))
        b.wait("pe", t_w)
        if prev_evac is None:
            if itp >= 1:
                b.wait("pe", t_sevacD[itp - 1])
            else:
                b.wait("pe", t_ke[0])
            b.wait("pe", t_xn_last)
        else:
            b.wait("pe", prev_evac)
        tmm = None
        for ts in tss:
            for dinc in range(DC):
                emit = b.op if (ts == tss[-1] and dinc == DC - 1) else b.opn
                tmm = emit("pe", lambda t, dc=dc, ts=ts, dinc=dinc, fh=fh, h=h: t.matmul(
                    PS[:, fh + (ts - 2 * h) * 512:fh + (ts - 2 * h + 1) * 512],
                    wbuf[:, dinc * D + dc * 128:dinc * D + (dc + 1) * 128],
                    xn[:, dinc * T + ts * 512:dinc * T + (ts + 1) * 512],
                    start=(dinc == 0), stop=(dinc == DC - 1), skip_group_check=True))
        b.wait("dve", tmm)
        ev = b.op("dve", lambda v, dc=dc, fh=fh, h=h, w=len(tss) * 512: v.tensor_copy(
            out=outbuf[:, dc * T + 2 * h * 512:dc * T + 2 * h * 512 + w],
            in_=PS[:, fh:fh + w]))
        tdict[dc] = ev
        return ev

    NH = (TS + 1) // 2
    weave_units = {i: [(kind, dc, h) for (kind, dc) in grp for h in range(NH)]
                   for i, grp in weave_plan.items()}

    thr_toks = [None, None]
    t_gred = []
    nW1 = (D * F) // (128 * 2048)
    nW2 = (F * D) // (128 * 2048)
    w1r3 = dr["w1T"].rearrange("(a p) f -> p a f", p=128)
    w2r3 = dr["w2T"].rearrange("(a p) f -> p a f", p=128)

    def emit_gamma_chunk(i):
        if i < nW1:
            r3, j, width = w1r3, i, F
        else:
            r3, j, width = w2r3, i - nW1, D
        if width >= 2048:
            a_, f0 = (j * 2048) // width, (j * 2048) % width
            srcap = r3[:, a_, f0:f0 + 2048]
        else:
            na = 2048 // width
            srcap = r3[:, j * na:(j + 1) * na, :]
        scr = scrA if i % 2 == 0 else scrB
        b.wait("sync", t_xn_last)
        b.wait("sync", t_r)
        b.wait("sync", t_lnv)
        b.wait("sync", t_bcr[nsl - 1])
        if i >= 2:
            b.wait("sync", t_gred[i - 2])
        td = b.dma("sync", lambda s, srcap=srcap, scr=scr: s.dma_start(
            out=scr[:, 0:2048], in_=srcap))
        b.wait("dve", td)
        t_gred.append(b.op("dve", lambda v, i=i, scr=scr: v.tensor_reduce(
            out=sb["gred"][:, i:i + 1], in_=scr[:, 0:2048], axis=AX.X, op=ALU.add,
            apply_absolute_value=True)))

    def emit_thr(lo, hi, nelem, thr_t, thrn_t, col, t_psum_free, wait_rden=False):
        for i in range(lo, hi):
            b.wait("pe", t_gred[i])
        b.wait("pe", t_ones128f)
        b.wait("pe", t_psum_free)
        tp = b.op("pe", lambda t: t.matmul(PS[0:1, 2048 + col:2048 + col + (hi - lo)],
                                           sb["ones128f"][:, :], sb["gred"][:, lo:hi],
                                           start=True, stop=True))
        b.wait("act", tp)
        if wait_rden:
            b.wait("act", t_rden)
        ta = b.op("act", lambda a: a.activation(out=arena1[0:1, 1024 + col:1024 + col + (hi - lo)],
                                                in_=PS[0:1, 2048 + col:2048 + col + (hi - lo)], func=AF.Copy))
        b.wait("dve", ta)
        tr = b.op("dve", lambda v: v.tensor_reduce(out=arena1[0:1, 1100 + col:1101 + col],
                                                   in_=arena1[0:1, 1024 + col:1024 + col + (hi - lo)],
                                                   axis=AX.X, op=ALU.add))
        b.wait("act", tr)
        b.wait("act", t_eps2)
        ts_ = b.op("act", lambda a: a.activation(out=arena1[0:1, 1102 + col:1103 + col],
                                                 in_=arena1[0:1, 1100 + col:1101 + col],
                                                 func=AF.Relu, scale=0.5 / nelem, bias=sb["eps2T"][0:1, 0:1]))
        b.wait("pe", ts_)
        tb_ = b.op("pe", lambda t: t.matmul(PS[:, 2148 + col:2149 + col], sb["onesf"][0:1, :],
                                            arena1[0:1, 1102 + col:1103 + col], start=True, stop=True))
        b.wait("act", tb_)
        tt = b.op("act", lambda a: a.activation(out=thr_t[:], in_=PS[:, 2148 + col:2149 + col], func=AF.Copy))
        b.wait("dve", tt)
        tn = b.op("dve", lambda v: v.tensor_single_scalar(out=thrn_t[:], in_=thr_t[:],
                                                          scalar=-1.0, op=ALU.mult))
        return tt, tn


    t_sc = {}
    t_exp = {}
    t_pv = {}
    t_sevacD = {}
    t_dD = {}
    NQS = T // 512
    NITP = (H // 2) * NQS

    def emit_scores(itp, kc):
        hp, qs = itp // NQS, itp % NQS
        gi = itp * KC + kc
        pbase = (kc % 2) * 1024
        if gi >= 2:
            b.wait("pe", t_exp[gi - 2])
        if kc == 0 and qs == 0:
            b.wait("pe", t_qe[hp])
            b.wait("pe", t_ke[hp])
        t_sc[gi] = None
        b.opn("pe", lambda t, kc=kc, pbase=pbase, hp=hp, qs=qs: t.matmul(
            PS[:, pbase:pbase + 512],
            sb["kb"][0:64, hp * T + kc * 128:hp * T + (kc + 1) * 128],
            sb["qb"][0:64, hp * T + qs * 512:hp * T + (qs + 1) * 512],
            start=True, stop=True, tile_position=(0, 0)))
        t_sc[gi] = b.op("pe", lambda t, kc=kc, pbase=pbase, hp=hp, qs=qs: t.matmul(
            PS[:, pbase + 512:pbase + 1024],
            sb["kb"][64:128, hp * T + kc * 128:hp * T + (kc + 1) * 128],
            sb["qb"][64:128, hp * T + qs * 512:hp * T + (qs + 1) * 512],
            start=True, stop=True, tile_position=(64, 0)))

    def emit_exp(itp, kc):
        qs = itp % NQS
        slot = kc % 4
        gi = itp * KC + kc
        base = (kc % 2) * 1024
        gk = (kc * 128) // G
        qlo, glo, ghi = qs * 512, gk * G, (gk + 1) * G
        ilo, ihi = max(qlo, glo) - qlo, min(qlo + 512, ghi) - qlo
        b.wait("act", t_sc[gi])
        if gi >= 4:
            b.wait("act", t_pv[gi - 4])
        if gi == 0:
            b.wait("act", t_scint)
            b.wait("act", t_scintra)
            b.wait("act", t_bias4)
        calls = []
        if ihi <= ilo:
            calls.append((0, 512, "inter"))
        else:
            if ilo > 0:
                calls.append((0, ilo, "inter"))
            calls.append((ilo, ihi, "intra"))
            if ihi < 512:
                calls.append((ihi, 512, "inter"))
        for (lo, hi, kind) in calls:
            sc_t = sb["sc_intra"] if kind == "intra" else sb["sc_inter"]
            if lo == 0 and hi == 512:
                inap = PS[:, base:base + 1024]
                outap = sb["expS"][:, slot * 1024:slot * 1024 + 1024]
            else:
                inap = PS[:, base:base + 1024].rearrange("p (h q) -> p h q", h=2)[:, :, lo:hi]
                outap = sb["expS"][:, slot * 1024:slot * 1024 + 1024].rearrange(
                    "p (h q) -> p h q", h=2)[:, :, lo:hi]
            t_exp[gi] = b.op("act", lambda a, inap=inap, outap=outap, sc_t=sc_t: a.activation(
                out=outap, in_=inap, func=AF.Exp, scale=sc_t[:, 0:1], bias=sb["biasm4"][:, 0:1]))

    def emit_pv(itp, kc):
        hp = itp // NQS
        slot = kc % 4
        gi = itp * KC + kc
        abase = 2048 + (itp % 2) * 1024
        b.wait("pe", t_exp[gi])
        if kc == 0:
            b.wait("pe", t_v_done)
            if itp == 0:
                b.wait("pe", t_ke[0])
            if itp >= 1 and (itp - 1) in t_weave_itp:
                b.wait("pe", t_weave_itp[itp - 1])
            if itp >= 2:
                b.wait("pe", t_sevacD[itp - 2])
        for hh in range(2):
            emit = b.op if hh == 1 else b.opn
            tpv = emit("pe", lambda t, kc=kc, hh=hh, slot=slot, hp=hp, abase=abase: t.matmul(
                PS[0:65, abase + hh * 512:abase + (hh + 1) * 512],
                sb["vt"][:, kc * (H * 65) + (2 * hp + hh) * 65:kc * (H * 65) + (2 * hp + hh) * 65 + 65],
                sb["expS"][:, slot * 1024 + hh * 512:slot * 1024 + (hh + 1) * 512],
                start=(kc == 0), stop=(kc == KC - 1), skip_group_check=True))
            if tpv is not None:
                t_pv[gi] = tpv

    def kc_mask(itp, kc):
        qs = itp % NQS
        gk = (kc * 128) // G
        qlo, glo, ghi = qs * 512, gk * G, (gk + 1) * G
        ilo, ihi = max(qlo, glo) - qlo, min(qlo + 512, ghi) - qlo
        return (ilo, ihi)

    def emit_exp_merged(itp, kc0):
        slot0 = kc0 % 4
        gi0 = itp * KC + kc0
        b.wait("act", t_sc[gi0 + 1])
        if gi0 + 1 >= 4:
            b.wait("act", t_pv[gi0 + 1 - 4])
        if gi0 == 0:
            b.wait("act", t_scint)
            b.wait("act", t_scintra)
            b.wait("act", t_bias4)
        tok = b.op("act", lambda a, slot0=slot0: a.activation(
            out=sb["expS"][:, slot0 * 1024:slot0 * 1024 + 2048],
            in_=PS[:, 0:2048],
            func=AF.Exp, scale=sb["sc_inter"][:, 0:1], bias=sb["biasm4"][:, 0:1]))
        t_exp[gi0] = tok
        t_exp[gi0 + 1] = tok

    NGC = nW1 + nW2

    def emit_tail(itp):
        hp, qs = itp // NQS, itp % NQS
        emit_pv(itp, KC - 2)
        emit_pv(itp, KC - 1)
        b.wait("dve", t_pv[itp * KC + KC - 1])
        drow = (itp % 2) * 32
        abase = 2048 + (itp % 2) * 1024
        if itp >= 2:
            b.wait("dve", t_dD[(itp - 2, 0)])
            b.wait("dve", t_dD[(itp - 2, 1)])
        for hh in range(2):
            b.op("dve", lambda v, hp=hp, qs=qs, hh=hh, abase=abase: v.tensor_copy(
                out=sraw[hh * 64:hh * 64 + 64, hp * T + qs * 512:hp * T + (qs + 1) * 512],
                in_=PS[0:64, abase + hh * 512:abase + (hh + 1) * 512]))
            t_sevacD[itp] = b.op("dve", lambda v, drow=drow, hh=hh, abase=abase: v.tensor_copy(
                out=arena1[drow:drow + 1, hh * 512:(hh + 1) * 512],
                in_=PS[64:65, abase + hh * 512:abase + (hh + 1) * 512]))
            b.wait("gpd", t_sevacD[itp])
            t_dD[(itp, hh)] = b.dma("gpd", lambda g, hp=hp, qs=qs, hh=hh, drow=drow: g.dma_start(
                out=Darena[(2 * hp + hh) * NQS + qs:(2 * hp + hh) * NQS + qs + 1, 0:512],
                in_=arena1[drow:drow + 1, hh * 512:(hh + 1) * 512]))

    for itp in range(NITP):
        hp, qs = itp // NQS, itp % NQS
        for i_g in range(itp * NGC // NITP, (itp + 1) * NGC // NITP):
            emit_gamma_chunk(i_g)
        for kc0 in range(0, KC, 2):
            emit_scores(itp, kc0)
            emit_scores(itp, kc0 + 1)
            emit_exp(itp, kc0)
            emit_exp(itp, kc0 + 1)
            if kc0 == 0 and itp >= 1:
                emit_tail(itp - 1)
            if NITP == 32 and NGC == 24 and itp % 2 == 1:
                if itp == 23 and kc0 == 0:
                    thr_toks[0] = emit_thr(0, nW1, D * F, sb["thr1"], sb["thrn1"],
                                           0, t_sevacD[itp - 1])
                if itp == NITP - 1 and kc0 == 8:
                    thr_toks[1] = emit_thr(nW1, nW1 + nW2, F * D, sb["thr2"],
                                           sb["thrn2"], 40, t_sevacD[itp - 1])
            if kc0 >= 2:
                emit_pv(itp, kc0 - 2)
                emit_pv(itp, kc0 - 1)
            units = weave_units.get(itp, [])
            nsplit = (len(units) + 1) // 2
            if kc0 == KC - 6 and units:
                pe_ = None
                for kind_, dc_, h_ in units[:nsplit]:
                    pe_ = emit_weave_half(itp, kind_, dc_, h_, pe_)
                t_weave_itp[itp] = pe_
            if kc0 == KC - 4 and units:
                pe_ = t_weave_itp.get(itp)
                for kind_, dc_, h_ in units[nsplit:]:
                    pe_ = emit_weave_half(itp, kind_, dc_, h_, pe_)
                t_weave_itp[itp] = pe_
    emit_tail(NITP - 1)

    NDROW = H * NQS
    for i in range(max(0, NITP - 2), NITP):
        for hh in range(2):
            b.wait("act", t_dD[(i, hh)])
    for tk in thr_toks:
        if tk is not None:
            b.wait("act", tk[1])
    t_ln_ = b.op("act", lambda a: a.activation(out=arena1[0:NDROW, 1024:1024 + 512],
                                               in_=Darena[0:NDROW, 0:512], func=AF.Ln))
    b.wait("act", t_ln_)
    t_rden = b.op("act", lambda a: a.activation(out=rdenA[64:64 + NDROW, 0:512],
                                                in_=arena1[0:NDROW, 1024:1024 + 512], func=AF.Exp, scale=-1.0))

    if thr_toks[0] is None:
        thr_toks[0] = emit_thr(0, nW1, D * F, sb["thr1"], sb["thrn1"], 0,
                               t_sevacD[NITP - 1], wait_rden=True)
    if thr_toks[1] is None:
        thr_toks[1] = emit_thr(nW1, nW1 + nW2, F * D, sb["thr2"], sb["thrn2"], 40,
                               t_sevacD[NITP - 1], wait_rden=True)
    t_thr1, t_thr2 = thr_toks

    last_sc = t_sc[NITP * KC - 1]
    last_exp = t_exp[NITP * KC - 1]
    last_pv = t_pv[NITP * KC - 1]
    t_qz1 = {}
    t_qz2 = {}
    t_f1 = {}
    t_f1e = {}
    t_f2 = {}
    t_f2e = {}
    t_silu = {}

    scr_slots = [(scrA, 0), (scrA, 1024), (scrB, 0), (scrB, 1024)]
    slot_free = [None] * 4

    def wslot(qi, dc):
        return sb["gacc"][:, ((dc - qi) % GSL) * T:(((dc - qi) % GSL) + 1) * T]

    def quantize(qi, which):
        toks = {}
        n = DC if which == 1 else FCQ
        for j in range(n):
            sl = j % 4
            scr, off = scr_slots[sl]
            if which == 1:
                src = dr["w1T"][128 * j:128 * (j + 1), qi * cfg.FQ:(qi + 1) * cfg.FQ]
                dst = sb["w1q"][:, j * cfg.FQ:(j + 1) * cfg.FQ]
                thr_t, thrn_t, tth = sb["thr1"], sb["thrn1"], t_thr1
                t_prev = t_f1.get((qi - 1, FCQ - 1, TS - 1))
            else:
                src = dr["w2T"][qi * cfg.FQ + 128 * j:qi * cfg.FQ + 128 * (j + 1), :]
                dst = sb["w2q"][:, j * D:(j + 1) * D]
                thr_t, thrn_t, tth = sb["thr2"], sb["thrn2"], t_thr2
                t_prev = t_f2.get((qi - 1, DC - 1, TS - 1))
            if slot_free[sl] is not None:
                b.wait("sync", slot_free[sl])
            else:
                b.wait("sync", t_gred[-1])
            b.wait("sync", last_sc)
            b.wait("sync", last_pv)
            w = src.shape[1]
            td = b.dma("sync", lambda s, src=src, scr=scr, off=off, w=w: s.dma_start(
                out=scr[:, off:off + w], in_=src))
            b.wait("dve", td)
            if j == 0:
                b.wait("dve", tth[0])
                b.wait("dve", tth[1])
            if toks:
                b.wait("dve", toks[max(toks)])
            elif quantize.last_tq[0] is not None:
                b.wait("dve", quantize.last_tq[0])
            if t_prev is not None:
                b.wait("dve", t_prev)
            tb = b.op("dve", lambda v, scr=scr, off=off, w=w, thrn_t=thrn_t: v.tensor_single_scalar(
                out=sb["btmp"][:, 0:w], in_=scr[:, off:off + w], scalar=thrn_t[:, 0:1], op=ALU.is_lt))
            b.wait("dve", tb)
            tq = b.op("dve", lambda v, scr=scr, off=off, w=w, dst=dst, thr_t=thr_t: v.scalar_tensor_tensor(
                out=dst[:, 0:w], in0=scr[:, off:off + w], scalar=thr_t[:, 0:1], in1=sb["btmp"][:, 0:w],
                op0=ALU.is_gt, op1=ALU.subtract))
            slot_free[sl] = tq
            toks[j] = tq
            quantize.last_tq[0] = tq
        return toks
    quantize.last_tq = [None]

    t_xc = {}
    t_rd = {}
    t_bc = {}
    t_u = {}
    t_t2p = {}
    t_t3 = {}
    t_out = {}
    NQS2 = T // 512

    def emit_combine(dc):
        par = dc % 2
        scr = scrA if par == 0 else scrB
        b.wait("sync", t_qz2[NQ - 1][FCQ - 1])
        if dc >= 2:
            b.wait("sync", t_t3[dc - 2])
        t_xc[dc] = b.dma("sync", lambda s, dc=dc, scr=scr: s.dma_start(
            out=scr[:, 0:T], in_=dr["xT"][dc * 128:(dc + 1) * 128, :]))
        prow = par * 32
        b.wait("gpd", t_rden)
        if dc >= 2:
            b.wait("gpd", t_bc[(dc - 2, nsl - 1)])
        t_rd[dc] = b.dma("gpd", lambda g, dc=dc, prow=prow: g.dma_start(
            out=rdenA[prow:prow + 2, 0:NQS2 * 512].rearrange("p (q c) -> p q c", q=NQS2),
            in_=rdenA[64 + 2 * NQS2 * dc:64 + 2 * NQS2 * (dc + 1), 0:512]))
        b.wait("pe", t_f2e[(NQ - 2, DC - 1)])
        if dc >= 1 and (dc - 1) in t_u:
            b.wait("pe", t_u[dc - 1])
        if dc < 2:
            b.wait("pe", t_E2)
        b.wait("pe", t_rd[dc])
        for s in range(nsl):
            t_bc[(dc, s)] = b.op("pe", lambda t, s=s, prow=prow: t.matmul(
                PS[:, 2048 + s * 512:2048 + (s + 1) * 512],
                sb["E2"][prow:prow + 2, :], rdenA[prow:prow + 2, s * 512:(s + 1) * 512],
                start=True, stop=True))
        ub = sb["ub0"] if par == 0 else sb["ub1"]
        for s in range(nsl):
            b.wait("dve", t_bc[(dc, s)])
        if dc >= 2:
            b.wait("dve", t_t2p[dc - 2])
        if dc == 0:
            b.wait("dve", t_thr1[0])
            b.wait("dve", t_thr2[0])
            b.wait("dve", t_dD[(NITP - 1, 0)])
            b.wait("dve", t_dD[(NITP - 1, 1)])
        t_u[dc] = b.op("dve", lambda v, dc=dc, ub=ub: v.tensor_tensor(
            out=ub[:, 0:T], in0=PS[:, 2048:2048 + T], in1=sraw[:, dc * T:(dc + 1) * T],
            op=ALU.mult))
        b.wait("dve", t_silu[dc])
        b.wait("dve", t_u[dc])
        if dc == 0:
            b.wait("dve", t_f1[(NQ - 1, FCQ - 1, TS - 1)])
        if dc >= 1:
            b.wait("dve", t_t3.get(dc - 1))
        t_t2p[dc] = b.op("dve", lambda v, dc=dc, ub=ub: v.tensor_tensor(
            out=sb["cbuf"][:, 0:T], in0=wslot(2, dc), in1=ub[:, 0:T], op=ALU.mult))
        b.wait("gpd", t_xc[dc])
        if dc >= 1:
            b.wait("gpd", t_out.get(dc - 1))
        b.wait("gpd", t_t2p[dc])
        t_t3[dc] = b.op("gpd", lambda g, dc=dc, scr=scr: g.tensor_tensor(
            out=sb["outst"][:, 0:T], in0=sb["cbuf"][:, 0:T], in1=scr[:, 0:T], op=ALU.add))
        b.wait("gpd", t_t3[dc])
        t_out[dc] = b.dma("gpd", lambda g, dc=dc: g.dma_start(
            out=dr["outT"][dc * 128:(dc + 1) * 128, :], in_=sb["outst"][:, 0:T]))

    for qi in range(NQ):
        if qi == 0:
            t_qz1[0] = quantize(0, 1)
            t_qz2[0] = quantize(0, 2)
        for fc in range(FCQ):
            base = (fc % 2) * 2048
            for dinc in range(DC):
                b.wait("pe", t_qz1[qi][dinc])
            if qi == 0 and fc < 2:
                b.wait("pe", last_exp)
                b.wait("pe", t_sevacD[NITP - 1])
            if fc >= 2:
                b.wait("pe", t_f1e[(qi, fc - 2)])
            elif qi > 0:
                b.wait("pe", t_f2e[(qi - 1, DC - 2 + (fc % 2))])
            for ts in range(TS):
                for dinc in range(DC):
                    emit = b.op if dinc == DC - 1 else b.opn
                    tok_ = emit("pe", lambda t, fc=fc, ts=ts, dinc=dinc, base=base: t.matmul(
                        PS[:, base + ts * 512:base + (ts + 1) * 512],
                        sb["w1q"][:, dinc * cfg.FQ + fc * 128:dinc * cfg.FQ + (fc + 1) * 128],
                        xn[:, dinc * T + ts * 512:dinc * T + (ts + 1) * 512],
                        start=(dinc == 0), stop=(dinc == DC - 1), skip_group_check=True))
                    if tok_ is not None:
                        t_f1[(qi, fc, ts)] = tok_
            for ts in range(TS):
                b.wait("act", t_f1[(qi, fc, ts)])
            if qi == 0 and fc == 0:
                b.wait("act", t_b1r)
            t_f1e[(qi, fc)] = b.op("act", lambda a, qi=qi, fc=fc, base=base: a.activation(
                out=sb["hq"][:, fc * T:(fc + 1) * T], in_=PS[:, base:base + T],
                func=AF.Relu, bias=sb["b1r"][:, qi * FCQ + fc:qi * FCQ + fc + 1]))
        if qi + 1 < NQ:
            t_qz1[qi + 1] = quantize(qi + 1, 1)
        for dc in range(DC):
            base = (dc % 2) * 2048 if qi < NQ - 1 else 0
            for fc in range(FCQ):
                b.wait("pe", t_qz2[qi][fc])
                b.wait("pe", t_f1e[(qi, fc)])
            if qi < NQ - 1:
                if dc >= 2:
                    b.wait("pe", t_f2e[(qi, dc - 2)])
                else:
                    b.wait("pe", t_f1e[(qi, FCQ - 2 + dc)])
            else:
                if dc >= 1:
                    b.wait("pe", t_f2e[(qi, dc - 1)])
                else:
                    b.wait("pe", t_f1e[(qi, FCQ - 2)])
                    b.wait("pe", t_f1e[(qi, FCQ - 1)])
            for ts in range(TS):
                for fc in range(FCQ):
                    emit = b.op if fc == FCQ - 1 else b.opn
                    tok_ = emit("pe", lambda t, dc=dc, ts=ts, fc=fc, base=base: t.matmul(
                        PS[:, base + ts * 512:base + (ts + 1) * 512],
                        sb["w2q"][:, fc * D + dc * 128:fc * D + (dc + 1) * 128],
                        sb["hq"][:, fc * T + ts * 512:fc * T + (ts + 1) * 512],
                        start=(fc == 0), stop=(fc == FCQ - 1), skip_group_check=True))
                    if tok_ is not None:
                        t_f2[(qi, dc, ts)] = tok_
            if qi == 0:
                for ts in range(TS):
                    b.wait("dve", t_f2[(qi, dc, ts)])
                b.wait("dve", t_b2r)
                te = b.op("dve", lambda v, dc=dc, base=base: v.tensor_single_scalar(
                    out=wslot(0, dc), in_=PS[:, base:base + T],
                    scalar=sb["b2r"][:, dc:dc + 1], op=ALU.add))
                t_f2e[(qi, dc)] = te
            else:
                for ts in range(TS):
                    b.wait("dve", t_f2[(qi, dc, ts)])
                b.wait("dve", t_f2e[(qi - 1, dc)])
                if dc >= 1:
                    b.wait("dve", t_f2e[(qi, dc - 1)])
                if qi < NQ - 1:
                    te = b.op("dve", lambda v, qi=qi, dc=dc, base=base: v.tensor_tensor(
                        out=wslot(qi, dc), in0=wslot(qi - 1, dc), in1=PS[:, base:base + T], op=ALU.add))
                    t_f2e[(qi, dc)] = te
                else:
                    if dc >= 1 and (dc - 1) in t_t2p:
                        b.wait("dve", t_t2p[dc - 1])
                    te = b.op("dve", lambda v, qi=qi, dc=dc, base=base: v.tensor_tensor(
                        out=wslot(3, dc), in0=wslot(2, dc), in1=PS[:, base:base + T], op=ALU.add))
                    t_f2e[(qi, dc)] = te
                    b.wait("act", te)
                    b.wait("act", t_f2e[(qi, dc)])
                    t_silu[dc] = b.op("act", lambda a, dc=dc: a.activation(
                        out=wslot(2, dc), in_=wslot(3, dc), func=AF.Silu))
                    emit_combine(dc)

        if qi + 1 < NQ:
            t_qz2[qi + 1] = quantize(qi + 1, 2)

    for dcw in range(max(0, DC - 4), DC):
        b.wait("gpd", t_out[dcw])



def _prep_inputs(cfg, x, Wq, Wk, Wv, ipw, W1, b1, W2, b2):
    import ml_dtypes
    T, D, F = cfg.T, cfg.D, cfg.F
    perm = perm_for(T)
    wqT = np.ascontiguousarray(np.asarray(Wq, dtype=np.float32).T)
    wkT = np.ascontiguousarray(np.asarray(Wk, dtype=np.float32).T)
    wvT = np.ascontiguousarray(np.asarray(Wv, dtype=np.float32).T)
    w1T = np.ascontiguousarray(np.asarray(W1, dtype=np.float32).T)
    w2T = np.ascontiguousarray(np.asarray(W2, dtype=np.float32).T)
    b1r = np.ascontiguousarray(np.asarray(b1, dtype=np.float32).reshape(F // 128, 128).T)
    b2r = np.ascontiguousarray(np.asarray(b2, dtype=np.float32).reshape(D // 128, 128).T)
    ipw_ = np.asarray(ipw, dtype=np.float32).reshape(1, 1)
    E2 = np.zeros((2, 128), dtype=ml_dtypes.bfloat16)
    E2[0, 0:64] = 1
    E2[1, 64:128] = 1
    x = np.asarray(x, dtype=np.float32)
    in_maps = []
    for c in range(x.shape[0]):
        xT = np.ascontiguousarray(x[c][perm].T)
        in_maps.append({"xT": xT, "wqT": wqT, "wkT": wkT, "wvT": wvT,
                       "w1T": w1T, "w2T": w2T, "b1r": b1r, "b2r": b2r,
                        "ipw": ipw_, "E2": E2})
    return in_maps, perm


def kernel(L0_attn_x, L0_attn_Wq, L0_attn_Wk, L0_attn_Wv, L0_attn_inter_palace_weight,
           L0_ffn_W1, L0_ffn_b1, L0_ffn_W2, L0_ffn_b2):
    cfg = Cfg(T=2048, D=1024, F=4096)
    in_maps, perm = _prep_inputs(cfg, L0_attn_x, L0_attn_Wq, L0_attn_Wk, L0_attn_Wv,
                                 L0_attn_inter_palace_weight, L0_ffn_W1, L0_ffn_b1,
                                 L0_ffn_W2, L0_ffn_b2)
    nc = build_nc(cfg)
    B = len(in_maps)
    res = run_bass_kernel_spmd(nc, in_maps, core_ids=list(range(B)))
    T, D = cfg.T, cfg.D
    out = np.empty((B, T, D), dtype=np.float32)
    for c in range(B):
        out[c][perm] = np.asarray(res.results[c]["outT"]).T
    return out


# version 29
# speedup vs baseline: 1.0054x; 1.0054x over previous
import contextlib
import numpy as np

import concourse.bass as bass
import concourse.mybir as mybir
from concourse.bass_utils import run_bass_kernel_spmd

F32 = mybir.dt.float32
BF16 = mybir.dt.bfloat16
AF = mybir.ActivationFunctionType
ALU = mybir.AluOpType
AX = mybir.AxisListType

ENGS = ("pe", "act", "dve", "sync", "gpd")
NDMA = {"sync": 8, "gpd": 4}
SEMS = list(ENGS) + [f"sync{i}" for i in range(8)] + [f"gpd{i}" for i in range(4)]


class Cfg:

    def __init__(self, T=2048, D=1024, F=4096):
        self.T, self.D, self.F = T, D, F
        self.HD = 64
        self.H = D // self.HD
        self.DC = D // 128
        self.KC = T // 128
        self.QW = min(1024, T)
        self.NQW = T // self.QW
        self.NIT = self.H * self.NQW
        self.G = T // 8
        assert self.G % 128 == 0
        self.FC = F // 128
        self.NQ = 4
        self.FQ = F // self.NQ
        self.FCQ = self.FQ // 128
        self.TS = T // 512
        assert self.NIT <= 32


def perm_for(T):
    pid = (np.arange(T) % 64) // 8
    return np.argsort(pid, kind="stable")


class Emitter:

    def __init__(self, sems, active, eng_obj):
        self.sems = sems
        self.active = active
        self.eng = eng_obj
        self.n = {s: 0 for s in SEMS}
        self.ndma = {"sync": 0, "gpd": 0}

    def wait(self, engname, tok):
        if tok is None:
            return
        sem, val = tok
        if val <= 0:
            return
        if engname == self.active:
            self.eng.wait_ge(self.sems[sem], val)

    def op(self, engname, fn):
        self.n[engname] += 1
        tok = (engname, self.n[engname])
        if engname == self.active:
            fn(self.eng).then_inc(self.sems[engname], 1)
        return tok

    def opn(self, engname, fn):
        if engname == self.active:
            fn(self.eng)
        return None

    def dma(self, queue, fn):
        i = self.ndma[queue]
        self.ndma[queue] = i + 1
        k = NDMA[queue]
        sem = f"{queue}{i % k}"
        self.wait(queue, (sem, 16 * (i // k)))
        self.n[sem] += 16
        tok = (sem, self.n[sem])
        if queue == self.active:
            fn(self.eng).then_inc(self.sems[sem], 16)
        return tok


def build_nc(cfg: Cfg):
    nc = bass.Bass("TRN2", target_bir_lowering=False)
    T, D, F, H = cfg.T, cfg.D, cfg.F, cfg.H
    DC, KC, QW = cfg.DC, cfg.KC, cfg.QW

    dr = {}
    dr["xT"] = nc.dram_tensor("xT", [D, T], F32, kind="ExternalInput").ap()
    dr["wqT"] = nc.dram_tensor("wqT", [D, D], F32, kind="ExternalInput").ap()
    dr["wkT"] = nc.dram_tensor("wkT", [D, D], F32, kind="ExternalInput").ap()
    dr["wvT"] = nc.dram_tensor("wvT", [D, D], F32, kind="ExternalInput").ap()
    dr["w1T"] = nc.dram_tensor("w1T", [D, F], F32, kind="ExternalInput").ap()
    dr["w2T"] = nc.dram_tensor("w2T", [F, D], F32, kind="ExternalInput").ap()
    dr["b1r"] = nc.dram_tensor("b1r", [128, F // 128], F32, kind="ExternalInput").ap()
    dr["b2r"] = nc.dram_tensor("b2r", [128, D // 128], F32, kind="ExternalInput").ap()
    dr["ipw"] = nc.dram_tensor("ipw", [1, 1], F32, kind="ExternalInput").ap()
    dr["E2"] = nc.dram_tensor("E2", [2, 128], BF16, kind="ExternalInput").ap()
    dr["outT"] = nc.dram_tensor("outT", [D, T], F32, kind="ExternalOutput").ap()

    ARENA = 206816
    arena = nc.alloc_sbuf_tensor("arena", [128, ARENA], mybir.dt.uint8).ap()
    sb = {}
    ends = []

    def alloc(name, shape, dtype, off):
        esz = 2 if dtype == BF16 else 4
        nbytes = shape[1] * esz
        sb[name] = arena[0:shape[0], off:off + nbytes].bitcast(dtype)
        ends.append(off + nbytes)
        return off + nbytes

    bpr = 2 * DC * T
    alloc("xn", [128, DC * T], BF16, 0)
    oB = bpr
    alloc("sraw", [128, DC * T], BF16, oB)
    alloc("wvb", [128, DC * D], BF16, oB)
    alloc("wkb", [128, DC * D], BF16, oB + 2 * DC * D)
    oC = oB + bpr
    alloc("scrA", [128, 2048], F32, oC)
    alloc("scrB", [128, 2048], F32, oC + 8192)
    oD = oC + 16384
    alloc("wqb", [128, DC * D], BF16, oD + 2 * bpr - 2 * DC * D)
    alloc("xf", [128, DC * T], F32, oD)
    alloc("xbf", [128, DC * T], BF16, oD + 4 * DC * T)
    alloc("xsq", [128, 2 * 2048], BF16, oD + 6 * DC * T)
    alloc("qb", [128, DC * T], BF16, oD)
    alloc("kb", [128, DC * T], BF16, oD + bpr)
    o_expS = oD + 2 * bpr + 2 * KC * H * 65
    o_expS = (o_expS + 31) & ~31
    alloc("vt", [128, KC * (H * 65) + 64], BF16, oD + 2 * bpr)
    alloc("expS", [128, 4 * QW], BF16, o_expS)
    GSL = DC + 1
    o_w1q = oD + 2 * GSL * T
    o_w2q = o_w1q + 2 * DC * cfg.FQ
    o_hq = o_w2q + 2 * cfg.FCQ * D
    o_bt = o_hq + 2 * cfg.FCQ * T
    alloc("gacc", [128, GSL * T], BF16, oD)
    alloc("w1q", [128, DC * cfg.FQ], BF16, o_w1q)
    alloc("w2q", [128, cfg.FCQ * D], BF16, o_w2q)
    alloc("hq", [128, cfg.FCQ * T], BF16, o_hq)
    alloc("btmp", [128, 1024], BF16, o_bt)
    alloc("cbuf", [128, T], F32, o_w1q)
    o_outst = o_w1q + 4 * T if 2 * DC * cfg.FQ >= 8 * T else o_bt + 2048
    alloc("outst", [128, T], F32, o_outst)

    oE = (max(ends) + 31) & ~31
    oE = alloc("b1r", [128, F // 128], F32, oE)
    oE = alloc("b2r", [128, DC], F32, oE)
    oE = alloc("E2", [128, 128], BF16, (oE + 31) & ~31)
    oE = alloc("onesf", [1, 128], F32, (oE + 31) & ~31)
    oE = alloc("ones128f", [128, 1], F32, (oE + 31) & ~31)
    oE = alloc("onesb", [128, 1], BF16, (oE + 31) & ~31)
    oE = alloc("sc_intra", [128, 1], F32, (oE + 31) & ~31)
    oE = alloc("sc_inter", [128, 1], F32, (oE + 31) & ~31)
    oE = alloc("biasm4", [128, 1], F32, (oE + 31) & ~31)
    oE = alloc("epsT", [128, 1], F32, (oE + 31) & ~31)
    oE = alloc("eps2T", [128, 1], F32, (oE + 31) & ~31)
    oE = alloc("thr1", [128, 1], F32, (oE + 31) & ~31)
    oE = alloc("thrn1", [128, 1], F32, (oE + 31) & ~31)
    oE = alloc("thr2", [128, 1], F32, (oE + 31) & ~31)
    oE = alloc("thrn2", [128, 1], F32, (oE + 31) & ~31)
    oE = alloc("ipwt", [1, 8], F32, (oE + 31) & ~31)
    oE = alloc("gred", [128, 32], F32, (oE + 31) & ~31)
    oE = alloc("arena1", [128, 2048], F32, (oE + 31) & ~31)
    alloc("ub0", [128, T], BF16, oE - 8192)
    alloc("ub1", [128, T], BF16, oE - 8192 + 2 * T)
    oE = alloc("Darena", [64, 512], F32, (oE + 31) & ~31)
    oE = alloc("rdenA", [128, 2048], BF16, (oE + 31) & ~31)
    assert oE <= 206816, f"SBUF over budget: {oE}"

    PS = nc.alloc_psum_tensor("PS", [128, 4096], F32).ap()

    es = contextlib.ExitStack()
    sems = {s: es.enter_context(nc.semaphore("s_" + s)) for s in SEMS}

    with nc.Block() as block:
        def run(active):
            def f(eng_obj):
                b = Emitter(sems, active, eng_obj)
                schedule(b, cfg, dr, sb, PS)
            return f
        block.tensor(run("pe"))
        block.scalar(run("act"))
        block.vector(run("dve"))
        block.sync(run("sync"))
        block.gpsimd(run("gpd"))
    es.close()
    return nc


def schedule(b: Emitter, cfg: Cfg, dr, sb, PS):
    T, D, F, H = cfg.T, cfg.D, cfg.F, cfg.H
    DC, KC, QW, NQW, NIT = cfg.DC, cfg.KC, cfg.QW, cfg.NQW, cfg.NIT
    TS, NQ, FCQ, G = cfg.TS, cfg.NQ, cfg.FCQ, cfg.G
    GSL = DC + 1
    xn, sraw = sb["xn"], sb["sraw"]
    PSB = PS.bitcast(BF16)
    scrA, scrB = sb["scrA"], sb["scrB"]
    arena1, Darena, rdenA = sb["arena1"], sb["Darena"], sb["rdenA"]
    nsl = T // 512

    t_E2 = b.dma("sync", lambda s: s.dma_start(out=sb["E2"][0:2, :], in_=dr["E2"][:]))
    t_E2 = b.dma("sync", lambda s: s.dma_start(out=sb["E2"][32:34, :], in_=dr["E2"][:]))
    t_b1r = b.dma("sync", lambda s: s.dma_start(out=sb["b1r"][:], in_=dr["b1r"][:]))
    t_b2r = b.dma("sync", lambda s: s.dma_start(out=sb["b2r"][:], in_=dr["b2r"][:]))
    t_ipw = b.dma("sync", lambda s: s.dma_start(out=sb["ipwt"][0:1, 0:1], in_=dr["ipw"][:]))

    t_onesf = b.op("dve", lambda v: v.memset(sb["onesf"][:], 1.0))
    t_ones128f = b.op("dve", lambda v: v.memset(sb["ones128f"][:], 1.0))
    t_onesb = b.op("dve", lambda v: v.memset(sb["onesb"][:], 1.0))
    t_scintra = b.op("dve", lambda v: v.memset(sb["sc_intra"][:], 0.125))
    t_bias4 = b.op("dve", lambda v: v.memset(sb["biasm4"][:], -4.0))
    t_eps = b.op("dve", lambda v: v.memset(sb["epsT"][:], 1e-5))
    t_eps2 = b.op("dve", lambda v: v.memset(sb["eps2T"][:], 0.5e-5))

    b.wait("act", t_ipw)
    t0 = b.op("act", lambda a: a.activation(out=sb["ipwt"][0:1, 1:2], in_=sb["ipwt"][0:1, 0:1],
                                            func=AF.Exp, scale=-1.0))
    b.wait("dve", t0)
    t1 = b.op("dve", lambda v: v.tensor_single_scalar(out=sb["ipwt"][0:1, 2:3], in_=sb["ipwt"][0:1, 1:2],
                                                      scalar=1.0, op=ALU.add))
    b.wait("dve", t1)
    t2 = b.op("dve", lambda v: v.reciprocal(out=sb["ipwt"][0:1, 3:4], in_=sb["ipwt"][0:1, 2:3]))
    b.wait("pe", t2)
    b.wait("pe", t_onesf)
    t3 = b.op("pe", lambda t: t.matmul(PS[:, 3072:3073], sb["onesf"][0:1, :], sb["ipwt"][0:1, 3:4],
                                       start=True, stop=True))
    b.wait("act", t3)
    t_scint = b.op("act", lambda a: a.activation(out=sb["sc_inter"][:], in_=PS[:, 3072:3073],
                                                 func=AF.Copy, scale=0.125))

    t_xf = [b.dma("sync", lambda s, c=c: s.dma_start(
        out=sb["xf"][:, c * T:(c + 1) * T], in_=dr["xT"][c * 128:(c + 1) * 128, :]))
        for c in range(DC)]

    t_xbf, t_sq = [None] * DC, [None] * DC
    t_sum = [None] * nsl
    t_ssq = [None] * nsl
    t_ssq_chunk = [None] * DC
    for c in range(DC):
        b.wait("dve", t_xf[c])
        t_xbf[c] = b.op("dve", lambda v, c=c: v.tensor_copy(
            out=sb["xbf"][:, c * T:(c + 1) * T], in_=sb["xf"][:, c * T:(c + 1) * T]))
        if c >= 2:
            b.wait("dve", t_ssq_chunk[c - 2])
        b.wait("dve", t_xbf[c])
        t_sq[c] = b.op("dve", lambda v, c=c: v.tensor_tensor(
            out=sb["xsq"][:, (c % 2) * 2048:(c % 2) * 2048 + T],
            in0=sb["xbf"][:, c * T:(c + 1) * T], in1=sb["xbf"][:, c * T:(c + 1) * T], op=ALU.mult))
        b.wait("pe", t_xbf[c])
        b.wait("pe", t_sq[c])
        if c == 0:
            b.wait("pe", t_onesb)
        for s in range(nsl):
            emit = b.op if (c == DC - 1 or s == nsl - 1) else b.opn
            t_sum[s] = emit("pe", lambda t, c=c, s=s: t.matmul(
                PS[0:1, s * 512:(s + 1) * 512], sb["onesb"][:, :],
                sb["xbf"][:, c * T + s * 512:c * T + (s + 1) * 512],
                start=(c == 0), stop=(c == DC - 1), skip_group_check=True))
            t_ssq[s] = emit("pe", lambda t, c=c, s=s: t.matmul(
                PS[32:33, s * 512:(s + 1) * 512], sb["onesb"][:, :],
                sb["xsq"][:, (c % 2) * 2048 + s * 512:(c % 2) * 2048 + (s + 1) * 512],
                start=(c == 0), stop=(c == DC - 1), skip_group_check=True))
        t_ssq_chunk[c] = t_ssq[nsl - 1]

    b.wait("dve", t_ssq_chunk[DC - 1])
    t_vones = b.op("dve", lambda v: v.memset(
        sb["vt"][:, 0:KC * H * 65].rearrange("p (g c) -> p g c", c=65)[:, :, 64:65], 1.0))

    for s in range(nsl):
        if t_sum[s] is not None:
            b.wait("act", t_sum[s])
        if t_ssq[s] is not None:
            b.wait("act", t_ssq[s])
    t_m = b.op("act", lambda a: a.activation(out=arena1[0:1, 0:T], in_=PS[0:1, 0:T],
                                             func=AF.Copy, scale=1.0 / D))
    t_msq = b.op("act", lambda a: a.activation(out=arena1[32:33, 0:T], in_=PS[32:33, 0:T],
                                               func=AF.Copy, scale=1.0 / D))
    b.wait("act", t_m)
    t_m2 = b.op("act", lambda a: a.activation(out=scrA[32:33, 0:T], in_=arena1[0:1, 0:T], func=AF.Square))
    b.wait("dve", t_m2)
    b.wait("dve", t_msq)
    t_var = b.op("dve", lambda v: v.tensor_tensor(out=scrB[32:33, 0:T], in0=arena1[32:33, 0:T],
                                                  in1=scrA[32:33, 0:T], op=ALU.subtract))
    b.wait("act", t_var)
    b.wait("act", t_eps)
    t_lnv = b.op("act", lambda a: a.activation(out=scrA[0:1, 0:T], in_=scrB[32:33, 0:T],
                                               func=AF.Ln, bias=sb["epsT"][0:1, 0:1]))
    b.wait("act", t_lnv)
    t_r = b.op("act", lambda a: a.activation(out=scrB[0:1, 0:T], in_=scrA[0:1, 0:T],
                                             func=AF.Exp, scale=-0.5))
    t_bcm = [None] * nsl
    t_bcr = [None] * nsl
    b.wait("pe", t_m)
    for s in range(nsl):
        t_bcm[s] = b.op("pe", lambda t, s=s: t.matmul(
            PS[:, 2048 + s * 512:2048 + (s + 1) * 512], sb["onesf"][0:1, :],
            arena1[0:1, s * 512:(s + 1) * 512], start=True, stop=True))
    b.wait("pe", t_r)
    for s in range(nsl):
        t_bcr[s] = b.op("pe", lambda t, s=s: t.matmul(
            PS[:, s * 512:(s + 1) * 512], sb["onesf"][0:1, :],
            scrB[0:1, s * 512:(s + 1) * 512], start=True, stop=True))
    pairs = [(c, s) for c in (DC - 2, DC - 1) for s in range(nsl)]
    pairs += [(c, s) for s in range(nsl) for c in range(DC - 2)]
    scr3 = [scrA[:, 0:512], scrB[:, 0:512], arena1[:, 0:512]]
    t_xn_cs = {}
    t_subs = {}
    pend = None
    seen_slices = set()
    for i, (c, s) in enumerate(pairs):
        scol = s * 512
        if s not in seen_slices:
            seen_slices.add(s)
            b.wait("dve", t_bcm[s])
            b.wait("dve", t_bcr[s])
        if i >= 3:
            b.wait("dve", t_xn_cs[pairs[i - 3]])
        t_subs[i] = b.op("dve", lambda v, c=c, scol=scol, i=i: v.tensor_tensor(
            out=scr3[i % 3][:], in0=sb["xf"][:, c * T + scol:c * T + scol + 512],
            in1=PS[:, 2048 + scol:2048 + scol + 512], op=ALU.subtract))
        if pend is not None:
            pc, ps = pairs[pend]
            b.wait("dve", t_subs[pend])
            t_xn_cs[(pc, ps)] = b.op("dve", lambda v, pc=pc, ps=ps, pend=pend: v.tensor_tensor(
                out=xn[:, pc * T + ps * 512:pc * T + ps * 512 + 512],
                in0=scr3[pend % 3][:], in1=PS[:, ps * 512:ps * 512 + 512], op=ALU.mult))
        pend = i
    pc, ps = pairs[pend]
    b.wait("dve", t_subs[pend])
    t_xn_cs[(pc, ps)] = b.op("dve", lambda v, pc=pc, ps=ps, pend=pend: v.tensor_tensor(
        out=xn[:, pc * T + ps * 512:pc * T + ps * 512 + 512],
        in0=scr3[pend % 3][:], in1=PS[:, ps * 512:ps * 512 + 512], op=ALU.mult))
    t_slice_done = [t_xn_cs[(DC - 3, s)] for s in range(nsl)]
    t_xn_last = t_slice_done[nsl - 1]

    def load_weight(wi, dram, dstbuf, extra_wait=None):
        if extra_wait is not None:
            b.wait("gpd", extra_wait)
        return b.dma("gpd", lambda g: g.dma_start(
            out=dstbuf[:], in_=dram.rearrange("(a p) d -> p a d", p=128)))

    t_wv = load_weight(0, dr["wvT"], sb["wvb"])
    t_wk = load_weight(2, dr["wkT"], sb["wkb"], extra_wait=t_xf[DC - 1])
    t_wq = load_weight(1, dr["wqT"], sb["wqb"],
                       extra_wait=t_xn_cs[(DC - 1, nsl - 1)])

    NDS = D // 512
    t_ve = {}
    t_vmm_last = None
    vcol = lambda tc, ds: (tc % 2) * 512 + ds * 2048
    for tc in range(KC):
        if tc >= 2:
            b.wait("pe", t_ve[(tc - 2, NDS - 1)])
            b.wait("pe", t_slice_done[(tc * 128) // 512])
        else:
            b.wait("pe", t_slice_done[min(tc, nsl - 1)])
        b.wait("pe", t_wv)
        for ds in range(NDS):
            tmm = None
            for dinc in range(DC):
                emit = b.op if dinc == DC - 1 else b.opn
                tmm = emit("pe", lambda t, tc=tc, ds=ds, dinc=dinc: t.matmul(
                    PS[:, vcol(tc, ds):vcol(tc, ds) + 512],
                    xn[:, dinc * T + tc * 128:dinc * T + (tc + 1) * 128],
                    sb["wvb"][:, dinc * D + ds * 512:dinc * D + (ds + 1) * 512],
                    start=(dinc == 0), stop=(dinc == DC - 1), skip_group_check=True))
            b.wait("act", tmm)
            b.wait("act", t_vones)
            t_ve[(tc, ds)] = b.op("act", lambda a, tc=tc, ds=ds: a.activation(
                out=sb["vt"][:, tc * (H * 65) + ds * 8 * 65:tc * (H * 65) + (ds * 8 + 8) * 65]
                    .rearrange("p (h c) -> p h c", h=8)[:, :, 0:64],
                in_=PS[:, vcol(tc, ds):vcol(tc, ds) + 512].rearrange("p (h c) -> p h c", h=8),
                func=AF.Copy))
            t_vmm_last = tmm
    t_v_done = t_ve[(KC - 1, NDS - 1)]

    def emit_proj_group(wbuf, t_w, outbuf, dc, base, waits):
        b.wait("pe", t_w)
        for w_ in waits:
            b.wait("pe", w_)
        tmm = None
        for ts in range(TS):
            for dinc in range(DC):
                emit = b.op if (ts == TS - 1 and dinc == DC - 1) else b.opn
                tmm = emit("pe", lambda t, dc=dc, ts=ts, dinc=dinc, base=base: t.matmul(
                    PS[:, base + ts * 512:base + (ts + 1) * 512],
                    wbuf[:, dinc * D + dc * 128:dinc * D + (dc + 1) * 128],
                    xn[:, dinc * T + ts * 512:dinc * T + (ts + 1) * 512],
                    start=(dinc == 0), stop=(dinc == DC - 1), skip_group_check=True))
        b.wait("dve", tmm)
        return b.op("dve", lambda v, dc=dc, base=base: v.tensor_copy(
            out=outbuf[:, dc * T:(dc + 1) * T], in_=PS[:, base:base + T]))

    t_qe = {0: emit_proj_group(sb["wqb"], t_wq, sb["qb"], 0, 0,
                               [t_ve[(KC - 1, NDS - 1)], t_xn_last])}
    t_ke = {0: emit_proj_group(sb["wkb"], t_wk, sb["kb"], 0, 2048,
                               [t_ve[(KC - 1, NDS - 1)], t_xn_last])}

    kcut = DC - (DC * D) // T
    if (DC, cfg.T // 512) == (8, 4):
        weave_plan = {0: [("q", 1)], 1: [("k", 1)], 2: [("q", 2)], 3: [("k", 2)],
                      4: [("q", 3)], 5: [("k", 3)], 6: [("q", 4)], 7: [("q", 5)],
                      8: [("q", 6)], 9: [("q", 7)], 10: [("k", 4)], 11: [("k", 5)],
                      12: [("k", 6)], 13: [("k", 7)]}
        assert kcut == 4
    elif (DC, cfg.T // 512) == (4, 2):
        weave_plan = {0: [("q", 1)], 1: [("k", 1)], 2: [("q", 2), ("q", 3)],
                      3: [("k", 2), ("k", 3)]}
        assert kcut == 2
    else:
        raise AssertionError("no weave plan for this config")
    t_weave_itp = {}

    def emit_weave_half(itp, kind, dc, h, prev_evac):
        wbuf, outbuf, t_w, tdict = ((sb["wqb"], sb["qb"], t_wq, t_qe) if kind == "q"
                                    else (sb["wkb"], sb["kb"], t_wk, t_ke))
        fh = 2048 + (1 - itp % 2) * 1024
        tss = list(range(2 * h, min(2 * h + 2, TS)))
        b.wait("pe", t_w)
        if prev_evac is None:
            if itp >= 1:
                b.wait("pe", t_sevacD[itp - 1])
            else:
                b.wait("pe", t_ke[0])
            b.wait("pe", t_xn_last)
        else:
            b.wait("pe", prev_evac)
        tmm = None
        for ts in tss:
            for dinc in range(DC):
                emit = b.op if (ts == tss[-1] and dinc == DC - 1) else b.opn
                tmm = emit("pe", lambda t, dc=dc, ts=ts, dinc=dinc, fh=fh, h=h: t.matmul(
                    PS[:, fh + (ts - 2 * h) * 512:fh + (ts - 2 * h + 1) * 512],
                    wbuf[:, dinc * D + dc * 128:dinc * D + (dc + 1) * 128],
                    xn[:, dinc * T + ts * 512:dinc * T + (ts + 1) * 512],
                    start=(dinc == 0), stop=(dinc == DC - 1), skip_group_check=True))
        b.wait("dve", tmm)
        ev = b.op("dve", lambda v, dc=dc, fh=fh, h=h, w=len(tss) * 512: v.tensor_copy(
            out=outbuf[:, dc * T + 2 * h * 512:dc * T + 2 * h * 512 + w],
            in_=PS[:, fh:fh + w]))
        tdict[dc] = ev
        return ev

    NH = (TS + 1) // 2
    weave_units = {i: [(kind, dc, h) for (kind, dc) in grp for h in range(NH)]
                   for i, grp in weave_plan.items()}

    thr_toks = [None, None]
    t_gred = []
    nW1 = (D * F) // (128 * 2048)
    nW2 = (F * D) // (128 * 2048)
    w1r3 = dr["w1T"].rearrange("(a p) f -> p a f", p=128)
    w2r3 = dr["w2T"].rearrange("(a p) f -> p a f", p=128)

    def emit_gamma_chunk(i):
        if i < nW1:
            r3, j, width = w1r3, i, F
        else:
            r3, j, width = w2r3, i - nW1, D
        if width >= 2048:
            a_, f0 = (j * 2048) // width, (j * 2048) % width
            srcap = r3[:, a_, f0:f0 + 2048]
        else:
            na = 2048 // width
            srcap = r3[:, j * na:(j + 1) * na, :]
        scr = scrA if i % 2 == 0 else scrB
        b.wait("sync", t_xn_last)
        b.wait("sync", t_r)
        b.wait("sync", t_lnv)
        b.wait("sync", t_bcr[nsl - 1])
        if i >= 2:
            b.wait("sync", t_gred[i - 2])
        td = b.dma("sync", lambda s, srcap=srcap, scr=scr: s.dma_start(
            out=scr[:, 0:2048], in_=srcap))
        b.wait("dve", td)
        t_gred.append(b.op("dve", lambda v, i=i, scr=scr: v.tensor_reduce(
            out=sb["gred"][:, i:i + 1], in_=scr[:, 0:2048], axis=AX.X, op=ALU.add,
            apply_absolute_value=True)))

    def emit_thr(lo, hi, nelem, thr_t, thrn_t, col, t_psum_free, wait_rden=False):
        for i in range(lo, hi):
            b.wait("pe", t_gred[i])
        b.wait("pe", t_ones128f)
        b.wait("pe", t_psum_free)
        tp = b.op("pe", lambda t: t.matmul(PS[0:1, 2048 + col:2048 + col + (hi - lo)],
                                           sb["ones128f"][:, :], sb["gred"][:, lo:hi],
                                           start=True, stop=True))
        b.wait("act", tp)
        if wait_rden:
            b.wait("act", t_rden)
        ta = b.op("act", lambda a: a.activation(out=arena1[0:1, 1024 + col:1024 + col + (hi - lo)],
                                                in_=PS[0:1, 2048 + col:2048 + col + (hi - lo)], func=AF.Copy))
        b.wait("dve", ta)
        tr = b.op("dve", lambda v: v.tensor_reduce(out=arena1[0:1, 1100 + col:1101 + col],
                                                   in_=arena1[0:1, 1024 + col:1024 + col + (hi - lo)],
                                                   axis=AX.X, op=ALU.add))
        b.wait("act", tr)
        b.wait("act", t_eps2)
        ts_ = b.op("act", lambda a: a.activation(out=arena1[0:1, 1102 + col:1103 + col],
                                                 in_=arena1[0:1, 1100 + col:1101 + col],
                                                 func=AF.Relu, scale=0.5 / nelem, bias=sb["eps2T"][0:1, 0:1]))
        b.wait("pe", ts_)
        tb_ = b.op("pe", lambda t: t.matmul(PS[:, 2148 + col:2149 + col], sb["onesf"][0:1, :],
                                            arena1[0:1, 1102 + col:1103 + col], start=True, stop=True))
        b.wait("act", tb_)
        tt = b.op("act", lambda a: a.activation(out=thr_t[:], in_=PS[:, 2148 + col:2149 + col], func=AF.Copy))
        b.wait("dve", tt)
        tn = b.op("dve", lambda v: v.tensor_single_scalar(out=thrn_t[:], in_=thr_t[:],
                                                          scalar=-1.0, op=ALU.mult))
        return tt, tn


    t_sc = {}
    t_exp = {}
    t_pv = {}
    t_sevacD = {}
    t_dD = {}
    NQS = T // 512
    NITP = (H // 2) * NQS

    def emit_scores(itp, kc):
        hp, qs = itp // NQS, itp % NQS
        gi = itp * KC + kc
        pbase = (kc % 2) * 1024
        if gi >= 2:
            b.wait("pe", t_exp[gi - 2])
        if kc == 0 and qs == 0:
            b.wait("pe", t_qe[hp])
            b.wait("pe", t_ke[hp])
        t_sc[gi] = None
        b.opn("pe", lambda t, kc=kc, pbase=pbase, hp=hp, qs=qs: t.matmul(
            PS[:, pbase:pbase + 512],
            sb["kb"][0:64, hp * T + kc * 128:hp * T + (kc + 1) * 128],
            sb["qb"][0:64, hp * T + qs * 512:hp * T + (qs + 1) * 512],
            start=True, stop=True, tile_position=(0, 0)))
        t_sc[gi] = b.op("pe", lambda t, kc=kc, pbase=pbase, hp=hp, qs=qs: t.matmul(
            PS[:, pbase + 512:pbase + 1024],
            sb["kb"][64:128, hp * T + kc * 128:hp * T + (kc + 1) * 128],
            sb["qb"][64:128, hp * T + qs * 512:hp * T + (qs + 1) * 512],
            start=True, stop=True, tile_position=(64, 0)))

    def emit_exp(itp, kc):
        qs = itp % NQS
        slot = kc % 4
        gi = itp * KC + kc
        base = (kc % 2) * 1024
        gk = (kc * 128) // G
        qlo, glo, ghi = qs * 512, gk * G, (gk + 1) * G
        ilo, ihi = max(qlo, glo) - qlo, min(qlo + 512, ghi) - qlo
        b.wait("act", t_sc[gi])
        if gi >= 4:
            b.wait("act", t_pv[gi - 4])
        if gi == 0:
            b.wait("act", t_scint)
            b.wait("act", t_scintra)
            b.wait("act", t_bias4)
        calls = []
        if ihi <= ilo:
            calls.append((0, 512, "inter"))
        else:
            if ilo > 0:
                calls.append((0, ilo, "inter"))
            calls.append((ilo, ihi, "intra"))
            if ihi < 512:
                calls.append((ihi, 512, "inter"))
        for (lo, hi, kind) in calls:
            sc_t = sb["sc_intra"] if kind == "intra" else sb["sc_inter"]
            if lo == 0 and hi == 512:
                inap = PS[:, base:base + 1024]
                outap = sb["expS"][:, slot * 1024:slot * 1024 + 1024]
            else:
                inap = PS[:, base:base + 1024].rearrange("p (h q) -> p h q", h=2)[:, :, lo:hi]
                outap = sb["expS"][:, slot * 1024:slot * 1024 + 1024].rearrange(
                    "p (h q) -> p h q", h=2)[:, :, lo:hi]
            t_exp[gi] = b.op("act", lambda a, inap=inap, outap=outap, sc_t=sc_t: a.activation(
                out=outap, in_=inap, func=AF.Exp, scale=sc_t[:, 0:1], bias=sb["biasm4"][:, 0:1]))

    def emit_pv(itp, kc):
        hp = itp // NQS
        slot = kc % 4
        gi = itp * KC + kc
        abase = 2048 + (itp % 2) * 1024
        b.wait("pe", t_exp[gi])
        if kc == 0:
            b.wait("pe", t_v_done)
            if itp == 0:
                b.wait("pe", t_ke[0])
            if itp >= 1 and (itp - 1) in t_weave_itp:
                b.wait("pe", t_weave_itp[itp - 1])
            if itp >= 2:
                b.wait("pe", t_sevacD[itp - 2])
        for hh in range(2):
            emit = b.op if hh == 1 else b.opn
            tpv = emit("pe", lambda t, kc=kc, hh=hh, slot=slot, hp=hp, abase=abase: t.matmul(
                PS[0:65, abase + hh * 512:abase + (hh + 1) * 512],
                sb["vt"][:, kc * (H * 65) + (2 * hp + hh) * 65:kc * (H * 65) + (2 * hp + hh) * 65 + 65],
                sb["expS"][:, slot * 1024 + hh * 512:slot * 1024 + (hh + 1) * 512],
                start=(kc == 0), stop=(kc == KC - 1), skip_group_check=True))
            if tpv is not None:
                t_pv[gi] = tpv

    def kc_mask(itp, kc):
        qs = itp % NQS
        gk = (kc * 128) // G
        qlo, glo, ghi = qs * 512, gk * G, (gk + 1) * G
        ilo, ihi = max(qlo, glo) - qlo, min(qlo + 512, ghi) - qlo
        return (ilo, ihi)

    def emit_exp_merged(itp, kc0):
        slot0 = kc0 % 4
        gi0 = itp * KC + kc0
        b.wait("act", t_sc[gi0 + 1])
        if gi0 + 1 >= 4:
            b.wait("act", t_pv[gi0 + 1 - 4])
        if gi0 == 0:
            b.wait("act", t_scint)
            b.wait("act", t_scintra)
            b.wait("act", t_bias4)
        tok = b.op("act", lambda a, slot0=slot0: a.activation(
            out=sb["expS"][:, slot0 * 1024:slot0 * 1024 + 2048],
            in_=PS[:, 0:2048],
            func=AF.Exp, scale=sb["sc_inter"][:, 0:1], bias=sb["biasm4"][:, 0:1]))
        t_exp[gi0] = tok
        t_exp[gi0 + 1] = tok

    NGC = nW1 + nW2

    def emit_tail(itp):
        hp, qs = itp // NQS, itp % NQS
        emit_pv(itp, KC - 2)
        emit_pv(itp, KC - 1)
        b.wait("dve", t_pv[itp * KC + KC - 1])
        drow = (itp % 2) * 32
        abase = 2048 + (itp % 2) * 1024
        if itp >= 2:
            b.wait("dve", t_dD[(itp - 2, 0)])
            b.wait("dve", t_dD[(itp - 2, 1)])
        for hh in range(2):
            b.op("dve", lambda v, hp=hp, qs=qs, hh=hh, abase=abase: v.tensor_copy(
                out=sraw[hh * 64:hh * 64 + 64, hp * T + qs * 512:hp * T + (qs + 1) * 512],
                in_=PS[0:64, abase + hh * 512:abase + (hh + 1) * 512]))
            t_sevacD[itp] = b.op("dve", lambda v, drow=drow, hh=hh, abase=abase: v.tensor_copy(
                out=arena1[drow:drow + 1, hh * 512:(hh + 1) * 512],
                in_=PS[64:65, abase + hh * 512:abase + (hh + 1) * 512]))
            b.wait("gpd", t_sevacD[itp])
            t_dD[(itp, hh)] = b.dma("gpd", lambda g, hp=hp, qs=qs, hh=hh, drow=drow: g.dma_start(
                out=Darena[(2 * hp + hh) * NQS + qs:(2 * hp + hh) * NQS + qs + 1, 0:512],
                in_=arena1[drow:drow + 1, hh * 512:(hh + 1) * 512]))

    for itp in range(NITP):
        hp, qs = itp // NQS, itp % NQS
        for i_g in range(itp * NGC // NITP, (itp + 1) * NGC // NITP):
            emit_gamma_chunk(i_g)
        for kc0 in range(0, KC, 2):
            emit_scores(itp, kc0)
            emit_scores(itp, kc0 + 1)
            emit_exp(itp, kc0)
            emit_exp(itp, kc0 + 1)
            if kc0 == 0 and itp >= 1:
                emit_tail(itp - 1)
            if NITP == 32 and NGC == 32 and itp % 2 == 1:
                if itp == 23 and kc0 == 0:
                    thr_toks[0] = emit_thr(0, nW1, D * F, sb["thr1"], sb["thrn1"],
                                           0, t_sevacD[itp - 1])
                if itp == NITP - 1 and kc0 == 8:
                    thr_toks[1] = emit_thr(nW1, nW1 + nW2, F * D, sb["thr2"],
                                           sb["thrn2"], 40, t_sevacD[itp - 1])
            if kc0 >= 2:
                emit_pv(itp, kc0 - 2)
                emit_pv(itp, kc0 - 1)
            units = weave_units.get(itp, [])
            nsplit = (len(units) + 1) // 2
            if kc0 == KC - 6 and units:
                pe_ = None
                for kind_, dc_, h_ in units[:nsplit]:
                    pe_ = emit_weave_half(itp, kind_, dc_, h_, pe_)
                t_weave_itp[itp] = pe_
            if kc0 == KC - 4 and units:
                pe_ = t_weave_itp.get(itp)
                for kind_, dc_, h_ in units[nsplit:]:
                    pe_ = emit_weave_half(itp, kind_, dc_, h_, pe_)
                t_weave_itp[itp] = pe_
    emit_tail(NITP - 1)

    NDROW = H * NQS
    for i in range(max(0, NITP - 2), NITP):
        for hh in range(2):
            b.wait("act", t_dD[(i, hh)])
    for tk in thr_toks:
        if tk is not None:
            b.wait("act", tk[1])
    t_ln_ = b.op("act", lambda a: a.activation(out=arena1[0:NDROW, 1024:1024 + 512],
                                               in_=Darena[0:NDROW, 0:512], func=AF.Ln))
    b.wait("act", t_ln_)
    t_rden = b.op("act", lambda a: a.activation(out=rdenA[64:64 + NDROW, 0:512],
                                                in_=arena1[0:NDROW, 1024:1024 + 512], func=AF.Exp, scale=-1.0))

    if thr_toks[0] is None:
        thr_toks[0] = emit_thr(0, nW1, D * F, sb["thr1"], sb["thrn1"], 0,
                               t_sevacD[NITP - 1], wait_rden=True)
    if thr_toks[1] is None:
        thr_toks[1] = emit_thr(nW1, nW1 + nW2, F * D, sb["thr2"], sb["thrn2"], 40,
                               t_sevacD[NITP - 1], wait_rden=True)
    t_thr1, t_thr2 = thr_toks

    last_sc = t_sc[NITP * KC - 1]
    last_exp = t_exp[NITP * KC - 1]
    last_pv = t_pv[NITP * KC - 1]
    t_qz1 = {}
    t_qz2 = {}
    t_f1 = {}
    t_f1e = {}
    t_f2 = {}
    t_f2e = {}
    t_silu = {}

    scr_slots = [(scrA, 0), (scrA, 1024), (scrB, 0), (scrB, 1024)]
    slot_free = [None] * 4

    def wslot(qi, dc):
        return sb["gacc"][:, ((dc - qi) % GSL) * T:(((dc - qi) % GSL) + 1) * T]

    def quantize(qi, which):
        toks = {}
        n = DC if which == 1 else FCQ
        for j in range(n):
            sl = j % 4
            scr, off = scr_slots[sl]
            if which == 1:
                src = dr["w1T"][128 * j:128 * (j + 1), qi * cfg.FQ:(qi + 1) * cfg.FQ]
                dst = sb["w1q"][:, j * cfg.FQ:(j + 1) * cfg.FQ]
                thr_t, thrn_t, tth = sb["thr1"], sb["thrn1"], t_thr1
                t_prev = t_f1.get((qi - 1, FCQ - 1, TS - 1))
            else:
                src = dr["w2T"][qi * cfg.FQ + 128 * j:qi * cfg.FQ + 128 * (j + 1), :]
                dst = sb["w2q"][:, j * D:(j + 1) * D]
                thr_t, thrn_t, tth = sb["thr2"], sb["thrn2"], t_thr2
                t_prev = t_f2.get((qi - 1, DC - 1, TS - 1))
            if slot_free[sl] is not None:
                b.wait("sync", slot_free[sl])
            else:
                b.wait("sync", t_gred[-1])
            b.wait("sync", last_sc)
            b.wait("sync", last_pv)
            w = src.shape[1]
            td = b.dma("sync", lambda s, src=src, scr=scr, off=off, w=w: s.dma_start(
                out=scr[:, off:off + w], in_=src))
            b.wait("dve", td)
            if j == 0:
                b.wait("dve", tth[0])
                b.wait("dve", tth[1])
            if toks:
                b.wait("dve", toks[max(toks)])
            elif quantize.last_tq[0] is not None:
                b.wait("dve", quantize.last_tq[0])
            if t_prev is not None:
                b.wait("dve", t_prev)
            tb = b.op("dve", lambda v, scr=scr, off=off, w=w, thrn_t=thrn_t: v.tensor_single_scalar(
                out=sb["btmp"][:, 0:w], in_=scr[:, off:off + w], scalar=thrn_t[:, 0:1], op=ALU.is_lt))
            b.wait("dve", tb)
            tq = b.op("dve", lambda v, scr=scr, off=off, w=w, dst=dst, thr_t=thr_t: v.scalar_tensor_tensor(
                out=dst[:, 0:w], in0=scr[:, off:off + w], scalar=thr_t[:, 0:1], in1=sb["btmp"][:, 0:w],
                op0=ALU.is_gt, op1=ALU.subtract))
            slot_free[sl] = tq
            toks[j] = tq
            quantize.last_tq[0] = tq
        return toks
    quantize.last_tq = [None]

    t_xc = {}
    t_rd = {}
    t_bc = {}
    t_u = {}
    t_t2p = {}
    t_t3 = {}
    t_out = {}
    NQS2 = T // 512

    def emit_combine(dc):
        par = dc % 2
        scr = scrA if par == 0 else scrB
        b.wait("sync", t_qz2[NQ - 1][FCQ - 1])
        if dc >= 2:
            b.wait("sync", t_t3[dc - 2])
        t_xc[dc] = b.dma("sync", lambda s, dc=dc, scr=scr: s.dma_start(
            out=scr[:, 0:T], in_=dr["xT"][dc * 128:(dc + 1) * 128, :]))
        prow = par * 32
        b.wait("gpd", t_rden)
        if dc >= 2:
            b.wait("gpd", t_bc[(dc - 2, nsl - 1)])
        t_rd[dc] = b.dma("gpd", lambda g, dc=dc, prow=prow: g.dma_start(
            out=rdenA[prow:prow + 2, 0:NQS2 * 512].rearrange("p (q c) -> p q c", q=NQS2),
            in_=rdenA[64 + 2 * NQS2 * dc:64 + 2 * NQS2 * (dc + 1), 0:512]))
        b.wait("pe", t_f2e[(NQ - 2, DC - 1)])
        if dc >= 1 and (dc - 1) in t_u:
            b.wait("pe", t_u[dc - 1])
        if dc < 2:
            b.wait("pe", t_E2)
        b.wait("pe", t_rd[dc])
        for s in range(nsl):
            t_bc[(dc, s)] = b.op("pe", lambda t, s=s, prow=prow: t.matmul(
                PS[:, 2048 + s * 512:2048 + (s + 1) * 512],
                sb["E2"][prow:prow + 2, :], rdenA[prow:prow + 2, s * 512:(s + 1) * 512],
                start=True, stop=True))
        ub = sb["ub0"] if par == 0 else sb["ub1"]
        for s in range(nsl):
            b.wait("dve", t_bc[(dc, s)])
        if dc >= 2:
            b.wait("dve", t_t2p[dc - 2])
        if dc == 0:
            b.wait("dve", t_thr1[0])
            b.wait("dve", t_thr2[0])
            b.wait("dve", t_dD[(NITP - 1, 0)])
            b.wait("dve", t_dD[(NITP - 1, 1)])
        t_u[dc] = b.op("dve", lambda v, dc=dc, ub=ub: v.tensor_tensor(
            out=ub[:, 0:T], in0=PS[:, 2048:2048 + T], in1=sraw[:, dc * T:(dc + 1) * T],
            op=ALU.mult))
        b.wait("dve", t_silu[dc])
        b.wait("dve", t_u[dc])
        if dc == 0:
            b.wait("dve", t_f1[(NQ - 1, FCQ - 1, TS - 1)])
        if dc >= 1:
            b.wait("dve", t_t3.get(dc - 1))
        t_t2p[dc] = b.op("dve", lambda v, dc=dc, ub=ub: v.tensor_tensor(
            out=sb["cbuf"][:, 0:T], in0=wslot(2, dc), in1=ub[:, 0:T], op=ALU.mult))
        b.wait("dve", t_xc[dc])
        if dc >= 1:
            b.wait("dve", t_out.get(dc - 1))
        b.wait("dve", t_t2p[dc])
        t_t3[dc] = b.op("dve", lambda v, dc=dc, scr=scr: v.tensor_tensor(
            out=sb["outst"][:, 0:T], in0=sb["cbuf"][:, 0:T], in1=scr[:, 0:T], op=ALU.add))
        b.wait("gpd", t_t3[dc])
        t_out[dc] = b.dma("gpd", lambda g, dc=dc: g.dma_start(
            out=dr["outT"][dc * 128:(dc + 1) * 128, :], in_=sb["outst"][:, 0:T]))

    for qi in range(NQ):
        if qi == 0:
            t_qz1[0] = quantize(0, 1)
            t_qz2[0] = quantize(0, 2)
        for fc in range(FCQ):
            base = (fc % 2) * 2048
            for dinc in range(DC):
                b.wait("pe", t_qz1[qi][dinc])
            if qi == 0 and fc < 2:
                b.wait("pe", last_exp)
                b.wait("pe", t_sevacD[NITP - 1])
            if fc >= 2:
                b.wait("pe", t_f1e[(qi, fc - 2)])
            elif qi > 0:
                b.wait("pe", t_f2e[(qi - 1, DC - 2 + (fc % 2))])
            for ts in range(TS):
                for dinc in range(DC):
                    emit = b.op if dinc == DC - 1 else b.opn
                    tok_ = emit("pe", lambda t, fc=fc, ts=ts, dinc=dinc, base=base: t.matmul(
                        PS[:, base + ts * 512:base + (ts + 1) * 512],
                        sb["w1q"][:, dinc * cfg.FQ + fc * 128:dinc * cfg.FQ + (fc + 1) * 128],
                        xn[:, dinc * T + ts * 512:dinc * T + (ts + 1) * 512],
                        start=(dinc == 0), stop=(dinc == DC - 1), skip_group_check=True))
                    if tok_ is not None:
                        t_f1[(qi, fc, ts)] = tok_
            for ts in range(TS):
                b.wait("act", t_f1[(qi, fc, ts)])
            if qi == 0 and fc == 0:
                b.wait("act", t_b1r)
            t_f1e[(qi, fc)] = b.op("act", lambda a, qi=qi, fc=fc, base=base: a.activation(
                out=sb["hq"][:, fc * T:(fc + 1) * T], in_=PS[:, base:base + T],
                func=AF.Relu, bias=sb["b1r"][:, qi * FCQ + fc:qi * FCQ + fc + 1]))
        if qi + 1 < NQ:
            t_qz1[qi + 1] = quantize(qi + 1, 1)
        for dc in range(DC):
            base = (dc % 2) * 2048 if qi < NQ - 1 else 0
            for fc in range(FCQ):
                b.wait("pe", t_qz2[qi][fc])
                b.wait("pe", t_f1e[(qi, fc)])
            if qi < NQ - 1:
                if dc >= 2:
                    b.wait("pe", t_f2e[(qi, dc - 2)])
                else:
                    b.wait("pe", t_f1e[(qi, FCQ - 2 + dc)])
            else:
                if dc >= 1:
                    b.wait("pe", t_f2e[(qi, dc - 1)])
                else:
                    b.wait("pe", t_f1e[(qi, FCQ - 2)])
                    b.wait("pe", t_f1e[(qi, FCQ - 1)])
            for ts in range(TS):
                for fc in range(FCQ):
                    emit = b.op if fc == FCQ - 1 else b.opn
                    tok_ = emit("pe", lambda t, dc=dc, ts=ts, fc=fc, base=base: t.matmul(
                        PS[:, base + ts * 512:base + (ts + 1) * 512],
                        sb["w2q"][:, fc * D + dc * 128:fc * D + (dc + 1) * 128],
                        sb["hq"][:, fc * T + ts * 512:fc * T + (ts + 1) * 512],
                        start=(fc == 0), stop=(fc == FCQ - 1), skip_group_check=True))
                    if tok_ is not None:
                        t_f2[(qi, dc, ts)] = tok_
            if qi == 0:
                for ts in range(TS):
                    b.wait("dve", t_f2[(qi, dc, ts)])
                b.wait("dve", t_b2r)
                te = b.op("dve", lambda v, dc=dc, base=base: v.tensor_single_scalar(
                    out=wslot(0, dc), in_=PS[:, base:base + T],
                    scalar=sb["b2r"][:, dc:dc + 1], op=ALU.add))
                t_f2e[(qi, dc)] = te
            else:
                for ts in range(TS):
                    b.wait("dve", t_f2[(qi, dc, ts)])
                b.wait("dve", t_f2e[(qi - 1, dc)])
                if dc >= 1:
                    b.wait("dve", t_f2e[(qi, dc - 1)])
                if qi < NQ - 1:
                    te = b.op("dve", lambda v, qi=qi, dc=dc, base=base: v.tensor_tensor(
                        out=wslot(qi, dc), in0=wslot(qi - 1, dc), in1=PS[:, base:base + T], op=ALU.add))
                    t_f2e[(qi, dc)] = te
                else:
                    if dc >= 1 and (dc - 1) in t_t2p:
                        b.wait("dve", t_t2p[dc - 1])
                    te = b.op("dve", lambda v, qi=qi, dc=dc, base=base: v.tensor_tensor(
                        out=wslot(3, dc), in0=wslot(2, dc), in1=PS[:, base:base + T], op=ALU.add))
                    t_f2e[(qi, dc)] = te
                    b.wait("act", te)
                    b.wait("act", t_f2e[(qi, dc)])
                    t_silu[dc] = b.op("act", lambda a, dc=dc: a.activation(
                        out=wslot(2, dc), in_=wslot(3, dc), func=AF.Silu))
                    emit_combine(dc)

        if qi + 1 < NQ:
            t_qz2[qi + 1] = quantize(qi + 1, 2)

    for dcw in range(max(0, DC - 4), DC):
        b.wait("gpd", t_out[dcw])



def _prep_inputs(cfg, x, Wq, Wk, Wv, ipw, W1, b1, W2, b2):
    import ml_dtypes
    T, D, F = cfg.T, cfg.D, cfg.F
    perm = perm_for(T)
    wqT = np.ascontiguousarray(np.asarray(Wq, dtype=np.float32).T)
    wkT = np.ascontiguousarray(np.asarray(Wk, dtype=np.float32).T)
    wvT = np.ascontiguousarray(np.asarray(Wv, dtype=np.float32).T)
    w1T = np.ascontiguousarray(np.asarray(W1, dtype=np.float32).T)
    w2T = np.ascontiguousarray(np.asarray(W2, dtype=np.float32).T)
    b1r = np.ascontiguousarray(np.asarray(b1, dtype=np.float32).reshape(F // 128, 128).T)
    b2r = np.ascontiguousarray(np.asarray(b2, dtype=np.float32).reshape(D // 128, 128).T)
    ipw_ = np.asarray(ipw, dtype=np.float32).reshape(1, 1)
    E2 = np.zeros((2, 128), dtype=ml_dtypes.bfloat16)
    E2[0, 0:64] = 1
    E2[1, 64:128] = 1
    x = np.asarray(x, dtype=np.float32)
    in_maps = []
    for c in range(x.shape[0]):
        xT = np.ascontiguousarray(x[c][perm].T)
        in_maps.append({"xT": xT, "wqT": wqT, "wkT": wkT, "wvT": wvT,
                       "w1T": w1T, "w2T": w2T, "b1r": b1r, "b2r": b2r,
                        "ipw": ipw_, "E2": E2})
    return in_maps, perm


def kernel(L0_attn_x, L0_attn_Wq, L0_attn_Wk, L0_attn_Wv, L0_attn_inter_palace_weight,
           L0_ffn_W1, L0_ffn_b1, L0_ffn_W2, L0_ffn_b2):
    cfg = Cfg(T=2048, D=1024, F=4096)
    in_maps, perm = _prep_inputs(cfg, L0_attn_x, L0_attn_Wq, L0_attn_Wk, L0_attn_Wv,
                                 L0_attn_inter_palace_weight, L0_ffn_W1, L0_ffn_b1,
                                 L0_ffn_W2, L0_ffn_b2)
    nc = build_nc(cfg)
    B = len(in_maps)
    res = run_bass_kernel_spmd(nc, in_maps, core_ids=list(range(B)))
    T, D = cfg.T, cfg.D
    out = np.empty((B, T, D), dtype=np.float32)
    for c in range(B):
        out[c][perm] = np.asarray(res.results[c]["outT"]).T
    return out


# version 30 (final)
# speedup vs baseline: 1.0233x; 1.0178x over previous
import contextlib
import numpy as np

import concourse.bass as bass
import concourse.mybir as mybir
from concourse.bass_utils import run_bass_kernel_spmd

F32 = mybir.dt.float32
BF16 = mybir.dt.bfloat16
AF = mybir.ActivationFunctionType
ALU = mybir.AluOpType
AX = mybir.AxisListType

ENGS = ("pe", "act", "dve", "sync", "gpd")
NDMA = {"sync": 8, "gpd": 4}
SEMS = list(ENGS) + [f"sync{i}" for i in range(8)] + [f"gpd{i}" for i in range(4)]


class Cfg:

    def __init__(self, T=2048, D=1024, F=4096):
        self.T, self.D, self.F = T, D, F
        self.HD = 64
        self.H = D // self.HD
        self.DC = D // 128
        self.KC = T // 128
        self.QW = min(1024, T)
        self.NQW = T // self.QW
        self.NIT = self.H * self.NQW
        self.G = T // 8
        assert self.G % 128 == 0
        self.FC = F // 128
        self.NQ = 4
        self.FQ = F // self.NQ
        self.FCQ = self.FQ // 128
        self.TS = T // 512
        assert self.NIT <= 32


def perm_for(T):
    pid = (np.arange(T) % 64) // 8
    return np.argsort(pid, kind="stable")


class Emitter:

    def __init__(self, sems, active, eng_obj):
        self.sems = sems
        self.active = active
        self.eng = eng_obj
        self.n = {s: 0 for s in SEMS}
        self.ndma = {"sync": 0, "gpd": 0}

    def wait(self, engname, tok):
        if tok is None:
            return
        sem, val = tok
        if val <= 0:
            return
        if engname == self.active:
            self.eng.wait_ge(self.sems[sem], val)

    def op(self, engname, fn):
        self.n[engname] += 1
        tok = (engname, self.n[engname])
        if engname == self.active:
            fn(self.eng).then_inc(self.sems[engname], 1)
        return tok

    def opn(self, engname, fn):
        if engname == self.active:
            fn(self.eng)
        return None

    def dma(self, queue, fn):
        i = self.ndma[queue]
        self.ndma[queue] = i + 1
        k = NDMA[queue]
        sem = f"{queue}{i % k}"
        self.wait(queue, (sem, 16 * (i // k)))
        self.n[sem] += 16
        tok = (sem, self.n[sem])
        if queue == self.active:
            fn(self.eng).then_inc(self.sems[sem], 16)
        return tok


def build_nc(cfg: Cfg):
    nc = bass.Bass("TRN2", target_bir_lowering=False)
    T, D, F, H = cfg.T, cfg.D, cfg.F, cfg.H
    DC, KC, QW = cfg.DC, cfg.KC, cfg.QW

    dr = {}
    dr["xT"] = nc.dram_tensor("xT", [D, T], F32, kind="ExternalInput").ap()
    dr["wqT"] = nc.dram_tensor("wqT", [D, D], F32, kind="ExternalInput").ap()
    dr["wkT"] = nc.dram_tensor("wkT", [D, D], F32, kind="ExternalInput").ap()
    dr["wvT"] = nc.dram_tensor("wvT", [D, D], F32, kind="ExternalInput").ap()
    dr["w1T"] = nc.dram_tensor("w1T", [D, F], F32, kind="ExternalInput").ap()
    dr["w2T"] = nc.dram_tensor("w2T", [F, D], F32, kind="ExternalInput").ap()
    dr["b1r"] = nc.dram_tensor("b1r", [128, F // 128], F32, kind="ExternalInput").ap()
    dr["b2r"] = nc.dram_tensor("b2r", [128, D // 128], F32, kind="ExternalInput").ap()
    dr["ipw"] = nc.dram_tensor("ipw", [1, 1], F32, kind="ExternalInput").ap()
    dr["E2"] = nc.dram_tensor("E2", [2, 128], BF16, kind="ExternalInput").ap()
    dr["outT"] = nc.dram_tensor("outT", [D, T], F32, kind="ExternalOutput").ap()

    ARENA = 206816
    arena = nc.alloc_sbuf_tensor("arena", [128, ARENA], mybir.dt.uint8).ap()
    sb = {}
    ends = []

    def alloc(name, shape, dtype, off):
        esz = 2 if dtype == BF16 else 4
        nbytes = shape[1] * esz
        sb[name] = arena[0:shape[0], off:off + nbytes].bitcast(dtype)
        ends.append(off + nbytes)
        return off + nbytes

    bpr = 2 * DC * T
    alloc("xn", [128, DC * T], BF16, 0)
    oB = bpr
    alloc("sraw", [128, DC * T], BF16, oB)
    alloc("wvb", [128, DC * D], BF16, oB)
    alloc("wkb", [128, DC * D], BF16, oB + 2 * DC * D)
    oC = oB + bpr
    alloc("scrA", [128, 2048], F32, oC)
    alloc("scrB", [128, 2048], F32, oC + 8192)
    oD = oC + 16384
    alloc("wqb", [128, DC * D], BF16, oD + 2 * bpr - 2 * DC * D)
    alloc("xf", [128, DC * T], F32, oD)
    alloc("xbf", [128, DC * T], BF16, oD + 4 * DC * T)
    alloc("xsq", [128, 2 * 2048], BF16, oD + 6 * DC * T)
    alloc("qb", [128, DC * T], BF16, oD)
    alloc("kb", [128, DC * T], BF16, oD + bpr)
    o_expS = oD + 2 * bpr + 2 * KC * H * 65
    o_expS = (o_expS + 31) & ~31
    alloc("vt", [128, KC * (H * 65) + 64], BF16, oD + 2 * bpr)
    alloc("expS", [128, 4 * QW], BF16, o_expS)
    GSL = DC + 1
    o_w1q = oD + 2 * GSL * T
    o_w2q = o_w1q + 2 * DC * cfg.FQ
    o_hq = o_w2q + 2 * cfg.FCQ * D
    o_bt = o_hq + 2 * cfg.FCQ * T
    alloc("gacc", [128, GSL * T], BF16, oD)
    alloc("w1q", [128, DC * cfg.FQ], BF16, o_w1q)
    alloc("w2q", [128, cfg.FCQ * D], BF16, o_w2q)
    alloc("hq", [128, cfg.FCQ * T], BF16, o_hq)
    alloc("btmp", [128, 1024], BF16, o_bt)
    alloc("cbuf", [128, T], F32, o_w1q)
    o_outst = o_w1q + 4 * T if 2 * DC * cfg.FQ >= 8 * T else o_bt + 2048
    alloc("outst", [128, T], F32, o_outst)

    oE = (max(ends) + 31) & ~31
    oE = alloc("b1r", [128, F // 128], F32, oE)
    oE = alloc("b2r", [128, DC], F32, oE)
    oE = alloc("E2", [128, 128], BF16, (oE + 31) & ~31)
    oE = alloc("onesf", [1, 128], F32, (oE + 31) & ~31)
    oE = alloc("ones128f", [128, 1], F32, (oE + 31) & ~31)
    oE = alloc("onesb", [128, 1], BF16, (oE + 31) & ~31)
    oE = alloc("sc_intra", [128, 1], F32, (oE + 31) & ~31)
    oE = alloc("sc_inter", [128, 1], F32, (oE + 31) & ~31)
    oE = alloc("biasm4", [128, 1], F32, (oE + 31) & ~31)
    oE = alloc("epsT", [128, 1], F32, (oE + 31) & ~31)
    oE = alloc("eps2T", [128, 1], F32, (oE + 31) & ~31)
    oE = alloc("thr1", [128, 1], F32, (oE + 31) & ~31)
    oE = alloc("thrn1", [128, 1], F32, (oE + 31) & ~31)
    oE = alloc("thr2", [128, 1], F32, (oE + 31) & ~31)
    oE = alloc("thrn2", [128, 1], F32, (oE + 31) & ~31)
    oE = alloc("ipwt", [1, 8], F32, (oE + 31) & ~31)
    oE = alloc("gred", [128, 32], F32, (oE + 31) & ~31)
    oE = alloc("arena1", [128, 2048], F32, (oE + 31) & ~31)
    alloc("ub0", [128, T], BF16, oE - 8192)
    alloc("ub1", [128, T], BF16, oE - 8192 + 2 * T)
    oE = alloc("Darena", [64, 512], F32, (oE + 31) & ~31)
    oE = alloc("rdenA", [128, 2048], BF16, (oE + 31) & ~31)
    assert oE <= 206816, f"SBUF over budget: {oE}"

    PS = nc.alloc_psum_tensor("PS", [128, 4096], F32).ap()

    es = contextlib.ExitStack()
    sems = {s: es.enter_context(nc.semaphore("s_" + s)) for s in SEMS}

    with nc.Block() as block:
        def run(active):
            def f(eng_obj):
                b = Emitter(sems, active, eng_obj)
                schedule(b, cfg, dr, sb, PS)
            return f
        block.tensor(run("pe"))
        block.scalar(run("act"))
        block.vector(run("dve"))
        block.sync(run("sync"))
        block.gpsimd(run("gpd"))
    es.close()
    return nc


def schedule(b: Emitter, cfg: Cfg, dr, sb, PS):
    T, D, F, H = cfg.T, cfg.D, cfg.F, cfg.H
    DC, KC, QW, NQW, NIT = cfg.DC, cfg.KC, cfg.QW, cfg.NQW, cfg.NIT
    TS, NQ, FCQ, G = cfg.TS, cfg.NQ, cfg.FCQ, cfg.G
    GSL = DC + 1
    xn, sraw = sb["xn"], sb["sraw"]
    PSB = PS.bitcast(BF16)
    scrA, scrB = sb["scrA"], sb["scrB"]
    arena1, Darena, rdenA = sb["arena1"], sb["Darena"], sb["rdenA"]
    nsl = T // 512

    t_E2 = b.dma("sync", lambda s: s.dma_start(out=sb["E2"][0:2, :], in_=dr["E2"][:]))
    t_E2 = b.dma("sync", lambda s: s.dma_start(out=sb["E2"][32:34, :], in_=dr["E2"][:]))
    t_b1r = b.dma("sync", lambda s: s.dma_start(out=sb["b1r"][:], in_=dr["b1r"][:]))
    t_b2r = b.dma("sync", lambda s: s.dma_start(out=sb["b2r"][:], in_=dr["b2r"][:]))
    t_ipw = b.dma("sync", lambda s: s.dma_start(out=sb["ipwt"][0:1, 0:1], in_=dr["ipw"][:]))

    t_onesf = b.op("dve", lambda v: v.memset(sb["onesf"][:], 1.0))
    t_ones128f = b.op("dve", lambda v: v.memset(sb["ones128f"][:], 1.0))
    t_onesb = b.op("dve", lambda v: v.memset(sb["onesb"][:], 1.0))
    t_scintra = b.op("dve", lambda v: v.memset(sb["sc_intra"][:], 0.125))
    t_bias4 = b.op("dve", lambda v: v.memset(sb["biasm4"][:], -4.0))
    t_eps = b.op("dve", lambda v: v.memset(sb["epsT"][:], 1e-5))
    t_eps2 = b.op("dve", lambda v: v.memset(sb["eps2T"][:], 0.5e-5))

    b.wait("act", t_ipw)
    t0 = b.op("act", lambda a: a.activation(out=sb["ipwt"][0:1, 1:2], in_=sb["ipwt"][0:1, 0:1],
                                            func=AF.Exp, scale=-1.0))
    b.wait("dve", t0)
    t1 = b.op("dve", lambda v: v.tensor_single_scalar(out=sb["ipwt"][0:1, 2:3], in_=sb["ipwt"][0:1, 1:2],
                                                      scalar=1.0, op=ALU.add))
    b.wait("dve", t1)
    t2 = b.op("dve", lambda v: v.reciprocal(out=sb["ipwt"][0:1, 3:4], in_=sb["ipwt"][0:1, 2:3]))
    b.wait("pe", t2)
    b.wait("pe", t_onesf)
    t3 = b.op("pe", lambda t: t.matmul(PS[:, 3072:3073], sb["onesf"][0:1, :], sb["ipwt"][0:1, 3:4],
                                       start=True, stop=True))
    b.wait("act", t3)
    t_scint = b.op("act", lambda a: a.activation(out=sb["sc_inter"][:], in_=PS[:, 3072:3073],
                                                 func=AF.Copy, scale=0.125))

    t_xf = [b.dma("sync", lambda s, c=c: s.dma_start(
        out=sb["xf"][:, c * T:(c + 1) * T], in_=dr["xT"][c * 128:(c + 1) * 128, :]))
        for c in range(DC)]

    t_xbf, t_sq = [None] * DC, [None] * DC
    t_sum = [None] * nsl
    t_ssq = [None] * nsl
    t_ssq_chunk = [None] * DC
    for c in range(DC):
        b.wait("dve", t_xf[c])
        t_xbf[c] = b.op("dve", lambda v, c=c: v.tensor_copy(
            out=sb["xbf"][:, c * T:(c + 1) * T], in_=sb["xf"][:, c * T:(c + 1) * T]))
        if c >= 2:
            b.wait("dve", t_ssq_chunk[c - 2])
        b.wait("dve", t_xbf[c])
        t_sq[c] = b.op("dve", lambda v, c=c: v.tensor_tensor(
            out=sb["xsq"][:, (c % 2) * 2048:(c % 2) * 2048 + T],
            in0=sb["xbf"][:, c * T:(c + 1) * T], in1=sb["xbf"][:, c * T:(c + 1) * T], op=ALU.mult))
        b.wait("pe", t_xbf[c])
        b.wait("pe", t_sq[c])
        if c == 0:
            b.wait("pe", t_onesb)
        for s in range(nsl):
            emit = b.op if (c == DC - 1 or s == nsl - 1) else b.opn
            t_sum[s] = emit("pe", lambda t, c=c, s=s: t.matmul(
                PS[0:1, s * 512:(s + 1) * 512], sb["onesb"][:, :],
                sb["xbf"][:, c * T + s * 512:c * T + (s + 1) * 512],
                start=(c == 0), stop=(c == DC - 1), skip_group_check=True))
            t_ssq[s] = emit("pe", lambda t, c=c, s=s: t.matmul(
                PS[32:33, s * 512:(s + 1) * 512], sb["onesb"][:, :],
                sb["xsq"][:, (c % 2) * 2048 + s * 512:(c % 2) * 2048 + (s + 1) * 512],
                start=(c == 0), stop=(c == DC - 1), skip_group_check=True))
        t_ssq_chunk[c] = t_ssq[nsl - 1]

    b.wait("dve", t_ssq_chunk[DC - 1])
    t_vones = b.op("dve", lambda v: v.memset(
        sb["vt"][:, 0:KC * H * 65].rearrange("p (g c) -> p g c", c=65)[:, :, 64:65], 1.0))

    for s in range(nsl):
        if t_sum[s] is not None:
            b.wait("act", t_sum[s])
        if t_ssq[s] is not None:
            b.wait("act", t_ssq[s])
    t_m = b.op("act", lambda a: a.activation(out=arena1[0:1, 0:T], in_=PS[0:1, 0:T],
                                             func=AF.Copy, scale=1.0 / D))
    t_msq = b.op("act", lambda a: a.activation(out=arena1[32:33, 0:T], in_=PS[32:33, 0:T],
                                               func=AF.Copy, scale=1.0 / D))
    b.wait("act", t_m)
    t_m2 = b.op("act", lambda a: a.activation(out=scrA[32:33, 0:T], in_=arena1[0:1, 0:T], func=AF.Square))
    b.wait("dve", t_m2)
    b.wait("dve", t_msq)
    t_var = b.op("dve", lambda v: v.tensor_tensor(out=scrB[32:33, 0:T], in0=arena1[32:33, 0:T],
                                                  in1=scrA[32:33, 0:T], op=ALU.subtract))
    b.wait("act", t_var)
    b.wait("act", t_eps)
    t_lnv = b.op("act", lambda a: a.activation(out=scrA[0:1, 0:T], in_=scrB[32:33, 0:T],
                                               func=AF.Ln, bias=sb["epsT"][0:1, 0:1]))
    b.wait("act", t_lnv)
    t_r = b.op("act", lambda a: a.activation(out=scrB[0:1, 0:T], in_=scrA[0:1, 0:T],
                                             func=AF.Exp, scale=-0.5))
    t_bcm = [None] * nsl
    t_bcr = [None] * nsl
    b.wait("pe", t_m)
    for s in range(nsl):
        t_bcm[s] = b.op("pe", lambda t, s=s: t.matmul(
            PS[:, 2048 + s * 512:2048 + (s + 1) * 512], sb["onesf"][0:1, :],
            arena1[0:1, s * 512:(s + 1) * 512], start=True, stop=True))
    b.wait("pe", t_r)
    for s in range(nsl):
        t_bcr[s] = b.op("pe", lambda t, s=s: t.matmul(
            PS[:, s * 512:(s + 1) * 512], sb["onesf"][0:1, :],
            scrB[0:1, s * 512:(s + 1) * 512], start=True, stop=True))
    pairs = [(c, s) for c in (DC - 2, DC - 1) for s in range(nsl)]
    pairs += [(c, s) for s in range(nsl) for c in range(DC - 2)]
    scr3 = [scrA[:, 0:512], scrB[:, 0:512], arena1[:, 0:512]]
    t_xn_cs = {}
    t_subs = {}
    pend = None
    seen_slices = set()
    for i, (c, s) in enumerate(pairs):
        scol = s * 512
        if s not in seen_slices:
            seen_slices.add(s)
            b.wait("dve", t_bcm[s])
            b.wait("dve", t_bcr[s])
        if i >= 3:
            b.wait("dve", t_xn_cs[pairs[i - 3]])
        t_subs[i] = b.op("dve", lambda v, c=c, scol=scol, i=i: v.tensor_tensor(
            out=scr3[i % 3][:], in0=sb["xf"][:, c * T + scol:c * T + scol + 512],
            in1=PS[:, 2048 + scol:2048 + scol + 512], op=ALU.subtract))
        if pend is not None:
            pc, ps = pairs[pend]
            b.wait("dve", t_subs[pend])
            t_xn_cs[(pc, ps)] = b.op("dve", lambda v, pc=pc, ps=ps, pend=pend: v.tensor_tensor(
                out=xn[:, pc * T + ps * 512:pc * T + ps * 512 + 512],
                in0=scr3[pend % 3][:], in1=PS[:, ps * 512:ps * 512 + 512], op=ALU.mult))
        pend = i
    pc, ps = pairs[pend]
    b.wait("dve", t_subs[pend])
    t_xn_cs[(pc, ps)] = b.op("dve", lambda v, pc=pc, ps=ps, pend=pend: v.tensor_tensor(
        out=xn[:, pc * T + ps * 512:pc * T + ps * 512 + 512],
        in0=scr3[pend % 3][:], in1=PS[:, ps * 512:ps * 512 + 512], op=ALU.mult))
    t_slice_done = [t_xn_cs[(DC - 3, s)] for s in range(nsl)]
    t_xn_last = t_slice_done[nsl - 1]

    def load_weight(wi, dram, dstbuf, extra_wait=None):
        if extra_wait is not None:
            b.wait("gpd", extra_wait)
        return b.dma("gpd", lambda g: g.dma_start(
            out=dstbuf[:], in_=dram.rearrange("(a p) d -> p a d", p=128)))

    t_wv = load_weight(0, dr["wvT"], sb["wvb"])
    t_wk = load_weight(2, dr["wkT"], sb["wkb"], extra_wait=t_xf[DC - 1])
    t_wq = load_weight(1, dr["wqT"], sb["wqb"],
                       extra_wait=t_xn_cs[(DC - 1, nsl - 1)])

    NDS = D // 512
    t_ve = {}
    t_vmm_last = None
    vcol = lambda tc, ds: (tc % 2) * 512 + ds * 2048
    for tc in range(KC):
        if tc >= 2:
            b.wait("pe", t_ve[(tc - 2, NDS - 1)])
            b.wait("pe", t_slice_done[(tc * 128) // 512])
        else:
            b.wait("pe", t_slice_done[min(tc, nsl - 1)])
        b.wait("pe", t_wv)
        for ds in range(NDS):
            tmm = None
            for dinc in range(DC):
                emit = b.op if dinc == DC - 1 else b.opn
                tmm = emit("pe", lambda t, tc=tc, ds=ds, dinc=dinc: t.matmul(
                    PS[:, vcol(tc, ds):vcol(tc, ds) + 512],
                    xn[:, dinc * T + tc * 128:dinc * T + (tc + 1) * 128],
                    sb["wvb"][:, dinc * D + ds * 512:dinc * D + (ds + 1) * 512],
                    start=(dinc == 0), stop=(dinc == DC - 1), skip_group_check=True))
            b.wait("act", tmm)
            b.wait("act", t_vones)
            t_ve[(tc, ds)] = b.op("act", lambda a, tc=tc, ds=ds: a.activation(
                out=sb["vt"][:, tc * (H * 65) + ds * 8 * 65:tc * (H * 65) + (ds * 8 + 8) * 65]
                    .rearrange("p (h c) -> p h c", h=8)[:, :, 0:64],
                in_=PS[:, vcol(tc, ds):vcol(tc, ds) + 512].rearrange("p (h c) -> p h c", h=8),
                func=AF.Copy))
            t_vmm_last = tmm
    t_v_done = t_ve[(KC - 1, NDS - 1)]

    def emit_proj_group(wbuf, t_w, outbuf, dc, base, waits):
        b.wait("pe", t_w)
        for w_ in waits:
            b.wait("pe", w_)
        tmm = None
        for ts in range(TS):
            for dinc in range(DC):
                emit = b.op if (ts == TS - 1 and dinc == DC - 1) else b.opn
                tmm = emit("pe", lambda t, dc=dc, ts=ts, dinc=dinc, base=base: t.matmul(
                    PS[:, base + ts * 512:base + (ts + 1) * 512],
                    wbuf[:, dinc * D + dc * 128:dinc * D + (dc + 1) * 128],
                    xn[:, dinc * T + ts * 512:dinc * T + (ts + 1) * 512],
                    start=(dinc == 0), stop=(dinc == DC - 1), skip_group_check=True))
        b.wait("dve", tmm)
        return b.op("dve", lambda v, dc=dc, base=base: v.tensor_copy(
            out=outbuf[:, dc * T:(dc + 1) * T], in_=PS[:, base:base + T]))

    t_qe = {0: emit_proj_group(sb["wqb"], t_wq, sb["qb"], 0, 0,
                               [t_ve[(KC - 1, NDS - 1)], t_xn_last])}
    t_ke = {0: emit_proj_group(sb["wkb"], t_wk, sb["kb"], 0, 2048,
                               [t_ve[(KC - 1, NDS - 1)], t_xn_last])}

    kcut = DC - (DC * D) // T
    if (DC, cfg.T // 512) == (8, 4):
        weave_plan = {0: [("q", 1)], 1: [("k", 1)], 2: [("q", 2)], 3: [("k", 2)],
                      4: [("q", 3)], 5: [("k", 3)], 6: [("q", 4)], 7: [("q", 5)],
                      8: [("q", 6)], 9: [("q", 7)], 10: [("k", 4)], 11: [("k", 5)],
                      12: [("k", 6)], 13: [("k", 7)]}
        assert kcut == 4
    elif (DC, cfg.T // 512) == (4, 2):
        weave_plan = {0: [("q", 1)], 1: [("k", 1)], 2: [("q", 2), ("q", 3)],
                      3: [("k", 2), ("k", 3)]}
        assert kcut == 2
    else:
        raise AssertionError("no weave plan for this config")
    t_weave_itp = {}

    def emit_weave_half(itp, kind, dc, h, prev_evac):
        wbuf, outbuf, t_w, tdict = ((sb["wqb"], sb["qb"], t_wq, t_qe) if kind == "q"
                                    else (sb["wkb"], sb["kb"], t_wk, t_ke))
        fh = 2048 + (1 - itp % 2) * 1024
        tss = list(range(2 * h, min(2 * h + 2, TS)))
        b.wait("pe", t_w)
        if prev_evac is None:
            if itp >= 1:
                b.wait("pe", t_sevacD[itp - 1])
            else:
                b.wait("pe", t_ke[0])
            b.wait("pe", t_xn_last)
        else:
            b.wait("pe", prev_evac)
        tmm = None
        for ts in tss:
            for dinc in range(DC):
                emit = b.op if (ts == tss[-1] and dinc == DC - 1) else b.opn
                tmm = emit("pe", lambda t, dc=dc, ts=ts, dinc=dinc, fh=fh, h=h: t.matmul(
                    PS[:, fh + (ts - 2 * h) * 512:fh + (ts - 2 * h + 1) * 512],
                    wbuf[:, dinc * D + dc * 128:dinc * D + (dc + 1) * 128],
                    xn[:, dinc * T + ts * 512:dinc * T + (ts + 1) * 512],
                    start=(dinc == 0), stop=(dinc == DC - 1), skip_group_check=True))
        b.wait("dve", tmm)
        ev = b.op("dve", lambda v, dc=dc, fh=fh, h=h, w=len(tss) * 512: v.tensor_copy(
            out=outbuf[:, dc * T + 2 * h * 512:dc * T + 2 * h * 512 + w],
            in_=PS[:, fh:fh + w]))
        tdict[dc] = ev
        return ev

    NH = (TS + 1) // 2
    weave_units = {i: [(kind, dc, h) for (kind, dc) in grp for h in range(NH)]
                   for i, grp in weave_plan.items()}

    thr_toks = [None, None]
    t_gred = []
    nW1 = (D * F) // (128 * 2048)
    nW2 = (F * D) // (128 * 2048)
    w1r3 = dr["w1T"].rearrange("(a p) f -> p a f", p=128)
    w2r3 = dr["w2T"].rearrange("(a p) f -> p a f", p=128)

    def emit_gamma_chunk(i):
        if i < nW1:
            r3, j, width = w1r3, i, F
        else:
            r3, j, width = w2r3, i - nW1, D
        if width >= 2048:
            a_, f0 = (j * 2048) // width, (j * 2048) % width
            srcap = r3[:, a_, f0:f0 + 2048]
        else:
            na = 2048 // width
            srcap = r3[:, j * na:(j + 1) * na, :]
        scr = scrA if i % 2 == 0 else scrB
        b.wait("sync", t_xn_last)
        b.wait("sync", t_r)
        b.wait("sync", t_lnv)
        b.wait("sync", t_bcr[nsl - 1])
        if i >= 2:
            b.wait("sync", t_gred[i - 2])
        td = b.dma("sync", lambda s, srcap=srcap, scr=scr: s.dma_start(
            out=scr[:, 0:2048], in_=srcap))
        b.wait("dve", td)
        t_gred.append(b.op("dve", lambda v, i=i, scr=scr: v.tensor_reduce(
            out=sb["gred"][:, i:i + 1], in_=scr[:, 0:2048], axis=AX.X, op=ALU.add,
            apply_absolute_value=True)))

    def emit_thr(lo, hi, nelem, thr_t, thrn_t, col, t_psum_free, wait_rden=False):
        for i in range(lo, hi):
            b.wait("pe", t_gred[i])
        b.wait("pe", t_ones128f)
        b.wait("pe", t_psum_free)
        tp = b.op("pe", lambda t: t.matmul(PS[0:1, 2048 + col:2048 + col + (hi - lo)],
                                           sb["ones128f"][:, :], sb["gred"][:, lo:hi],
                                           start=True, stop=True))
        b.wait("act", tp)
        if wait_rden:
            b.wait("act", t_rden)
        ta = b.op("act", lambda a: a.activation(out=arena1[0:1, 1024 + col:1024 + col + (hi - lo)],
                                                in_=PS[0:1, 2048 + col:2048 + col + (hi - lo)], func=AF.Copy))
        b.wait("dve", ta)
        tr = b.op("dve", lambda v: v.tensor_reduce(out=arena1[0:1, 1100 + col:1101 + col],
                                                   in_=arena1[0:1, 1024 + col:1024 + col + (hi - lo)],
                                                   axis=AX.X, op=ALU.add))
        b.wait("act", tr)
        b.wait("act", t_eps2)
        ts_ = b.op("act", lambda a: a.activation(out=arena1[0:1, 1102 + col:1103 + col],
                                                 in_=arena1[0:1, 1100 + col:1101 + col],
                                                 func=AF.Relu, scale=0.5 / nelem, bias=sb["eps2T"][0:1, 0:1]))
        b.wait("pe", ts_)
        tb_ = b.op("pe", lambda t: t.matmul(PS[:, 2148 + col:2149 + col], sb["onesf"][0:1, :],
                                            arena1[0:1, 1102 + col:1103 + col], start=True, stop=True))
        b.wait("act", tb_)
        tt = b.op("act", lambda a: a.activation(out=thr_t[:], in_=PS[:, 2148 + col:2149 + col], func=AF.Copy))
        b.wait("dve", tt)
        tn = b.op("dve", lambda v: v.tensor_single_scalar(out=thrn_t[:], in_=thr_t[:],
                                                          scalar=-1.0, op=ALU.mult))
        return tt, tn


    t_qz1 = {}
    t_qz2 = {}
    t_f1 = {}
    t_f1e = {}
    t_f2 = {}
    t_f2e = {}
    t_silu = {}

    scr_slots = [(scrA, 0), (scrA, 1024), (scrB, 0), (scrB, 1024)]
    slot_free = [None] * 4

    def wslot(qi, dc):
        return sb["gacc"][:, ((dc - qi) % GSL) * T:(((dc - qi) % GSL) + 1) * T]

    def quantize(qi, which, early=False):
        toks = {}
        n = DC if which == 1 else FCQ
        for j in range(n):
            sl = j % 4
            scr, off = scr_slots[sl]
            if which == 1:
                src = dr["w1T"][128 * j:128 * (j + 1), qi * cfg.FQ:(qi + 1) * cfg.FQ]
                dst = sb["w1q"][:, j * cfg.FQ:(j + 1) * cfg.FQ]
                thr_t, thrn_t, tth = sb["thr1"], sb["thrn1"], thr_toks[0]
                t_prev = t_f1.get((qi - 1, FCQ - 1, TS - 1))
            else:
                src = dr["w2T"][qi * cfg.FQ + 128 * j:qi * cfg.FQ + 128 * (j + 1), :]
                dst = sb["w2q"][:, j * D:(j + 1) * D]
                thr_t, thrn_t, tth = sb["thr2"], sb["thrn2"], thr_toks[1]
                t_prev = t_f2.get((qi - 1, DC - 1, TS - 1))
            if slot_free[sl] is not None:
                b.wait("sync", slot_free[sl])
            else:
                b.wait("sync", t_gred[-1])
            if not early:
                b.wait("sync", last_sc)
                b.wait("sync", last_pv)
            w = src.shape[1]
            td = b.dma("sync", lambda s, src=src, scr=scr, off=off, w=w: s.dma_start(
                out=scr[:, off:off + w], in_=src))
            b.wait("dve", td)
            if j == 0:
                b.wait("dve", tth[0])
                b.wait("dve", tth[1])
            if toks:
                b.wait("dve", toks[max(toks)])
            elif quantize.last_tq[0] is not None:
                b.wait("dve", quantize.last_tq[0])
            if t_prev is not None:
                b.wait("dve", t_prev)
            btmp = (arena1[:, 1536:2048].bitcast(BF16) if early else sb["btmp"])
            tb = b.op("dve", lambda v, scr=scr, off=off, w=w, thrn_t=thrn_t, btmp=btmp: v.tensor_single_scalar(
                out=btmp[:, 0:w], in_=scr[:, off:off + w], scalar=thrn_t[:, 0:1], op=ALU.is_lt))
            b.wait("dve", tb)
            tq = b.op("dve", lambda v, scr=scr, off=off, w=w, dst=dst, thr_t=thr_t, btmp=btmp: v.scalar_tensor_tensor(
                out=dst[:, 0:w], in0=scr[:, off:off + w], scalar=thr_t[:, 0:1], in1=btmp[:, 0:w],
                op0=ALU.is_gt, op1=ALU.subtract))
            slot_free[sl] = tq
            toks[j] = tq
            quantize.last_tq[0] = tq
        return toks
    quantize.last_tq = [None]

    t_sc = {}
    t_exp = {}
    t_pv = {}
    t_sevacD = {}
    t_dD = {}
    NQS = T // 512
    NITP = (H // 2) * NQS

    def emit_scores(itp, kc):
        hp, qs = itp // NQS, itp % NQS
        gi = itp * KC + kc
        pbase = (kc % 2) * 1024
        if gi >= 2:
            b.wait("pe", t_exp[gi - 2])
        if kc == 0 and qs == 0:
            b.wait("pe", t_qe[hp])
            b.wait("pe", t_ke[hp])
        t_sc[gi] = None
        b.opn("pe", lambda t, kc=kc, pbase=pbase, hp=hp, qs=qs: t.matmul(
            PS[:, pbase:pbase + 512],
            sb["kb"][0:64, hp * T + kc * 128:hp * T + (kc + 1) * 128],
            sb["qb"][0:64, hp * T + qs * 512:hp * T + (qs + 1) * 512],
            start=True, stop=True, tile_position=(0, 0)))
        t_sc[gi] = b.op("pe", lambda t, kc=kc, pbase=pbase, hp=hp, qs=qs: t.matmul(
            PS[:, pbase + 512:pbase + 1024],
            sb["kb"][64:128, hp * T + kc * 128:hp * T + (kc + 1) * 128],
            sb["qb"][64:128, hp * T + qs * 512:hp * T + (qs + 1) * 512],
            start=True, stop=True, tile_position=(64, 0)))

    def emit_exp(itp, kc):
        qs = itp % NQS
        slot = kc % 4
        gi = itp * KC + kc
        base = (kc % 2) * 1024
        gk = (kc * 128) // G
        qlo, glo, ghi = qs * 512, gk * G, (gk + 1) * G
        ilo, ihi = max(qlo, glo) - qlo, min(qlo + 512, ghi) - qlo
        b.wait("act", t_sc[gi])
        if gi >= 4:
            b.wait("act", t_pv[gi - 4])
        if gi == 0:
            b.wait("act", t_scint)
            b.wait("act", t_scintra)
            b.wait("act", t_bias4)
        calls = []
        if ihi <= ilo:
            calls.append((0, 512, "inter"))
        else:
            if ilo > 0:
                calls.append((0, ilo, "inter"))
            calls.append((ilo, ihi, "intra"))
            if ihi < 512:
                calls.append((ihi, 512, "inter"))
        for (lo, hi, kind) in calls:
            sc_t = sb["sc_intra"] if kind == "intra" else sb["sc_inter"]
            if lo == 0 and hi == 512:
                inap = PS[:, base:base + 1024]
                outap = sb["expS"][:, slot * 1024:slot * 1024 + 1024]
            else:
                inap = PS[:, base:base + 1024].rearrange("p (h q) -> p h q", h=2)[:, :, lo:hi]
                outap = sb["expS"][:, slot * 1024:slot * 1024 + 1024].rearrange(
                    "p (h q) -> p h q", h=2)[:, :, lo:hi]
            t_exp[gi] = b.op("act", lambda a, inap=inap, outap=outap, sc_t=sc_t: a.activation(
                out=outap, in_=inap, func=AF.Exp, scale=sc_t[:, 0:1], bias=sb["biasm4"][:, 0:1]))

    def emit_pv(itp, kc):
        hp = itp // NQS
        slot = kc % 4
        gi = itp * KC + kc
        abase = 2048 + (itp % 2) * 1024
        b.wait("pe", t_exp[gi])
        if kc == 0:
            b.wait("pe", t_v_done)
            if itp == 0:
                b.wait("pe", t_ke[0])
            if itp >= 1 and (itp - 1) in t_weave_itp:
                b.wait("pe", t_weave_itp[itp - 1])
            if itp >= 2:
                b.wait("pe", t_sevacD[itp - 2])
        for hh in range(2):
            emit = b.op if hh == 1 else b.opn
            tpv = emit("pe", lambda t, kc=kc, hh=hh, slot=slot, hp=hp, abase=abase: t.matmul(
                PS[0:65, abase + hh * 512:abase + (hh + 1) * 512],
                sb["vt"][:, kc * (H * 65) + (2 * hp + hh) * 65:kc * (H * 65) + (2 * hp + hh) * 65 + 65],
                sb["expS"][:, slot * 1024 + hh * 512:slot * 1024 + (hh + 1) * 512],
                start=(kc == 0), stop=(kc == KC - 1), skip_group_check=True))
            if tpv is not None:
                t_pv[gi] = tpv

    def kc_mask(itp, kc):
        qs = itp % NQS
        gk = (kc * 128) // G
        qlo, glo, ghi = qs * 512, gk * G, (gk + 1) * G
        ilo, ihi = max(qlo, glo) - qlo, min(qlo + 512, ghi) - qlo
        return (ilo, ihi)

    def emit_exp_merged(itp, kc0):
        slot0 = kc0 % 4
        gi0 = itp * KC + kc0
        b.wait("act", t_sc[gi0 + 1])
        if gi0 + 1 >= 4:
            b.wait("act", t_pv[gi0 + 1 - 4])
        if gi0 == 0:
            b.wait("act", t_scint)
            b.wait("act", t_scintra)
            b.wait("act", t_bias4)
        tok = b.op("act", lambda a, slot0=slot0: a.activation(
            out=sb["expS"][:, slot0 * 1024:slot0 * 1024 + 2048],
            in_=PS[:, 0:2048],
            func=AF.Exp, scale=sb["sc_inter"][:, 0:1], bias=sb["biasm4"][:, 0:1]))
        t_exp[gi0] = tok
        t_exp[gi0 + 1] = tok

    NGC = nW1 + nW2

    def emit_tail(itp):
        hp, qs = itp // NQS, itp % NQS
        emit_pv(itp, KC - 2)
        emit_pv(itp, KC - 1)
        b.wait("dve", t_pv[itp * KC + KC - 1])
        drow = (itp % 2) * 32
        abase = 2048 + (itp % 2) * 1024
        if itp >= 2:
            b.wait("dve", t_dD[(itp - 2, 0)])
            b.wait("dve", t_dD[(itp - 2, 1)])
        for hh in range(2):
            b.op("dve", lambda v, hp=hp, qs=qs, hh=hh, abase=abase: v.tensor_copy(
                out=sraw[hh * 64:hh * 64 + 64, hp * T + qs * 512:hp * T + (qs + 1) * 512],
                in_=PS[0:64, abase + hh * 512:abase + (hh + 1) * 512]))
            t_sevacD[itp] = b.op("dve", lambda v, drow=drow, hh=hh, abase=abase: v.tensor_copy(
                out=arena1[drow:drow + 1, hh * 512:(hh + 1) * 512],
                in_=PS[64:65, abase + hh * 512:abase + (hh + 1) * 512]))
            b.wait("gpd", t_sevacD[itp])
            t_dD[(itp, hh)] = b.dma("gpd", lambda g, hp=hp, qs=qs, hh=hh, drow=drow: g.dma_start(
                out=Darena[(2 * hp + hh) * NQS + qs:(2 * hp + hh) * NQS + qs + 1, 0:512],
                in_=arena1[drow:drow + 1, hh * 512:(hh + 1) * 512]))

    for itp in range(NITP):
        hp, qs = itp // NQS, itp % NQS
        for i_g in range(itp * NGC // NITP, (itp + 1) * NGC // NITP):
            emit_gamma_chunk(i_g)
        for kc0 in range(0, KC, 2):
            emit_scores(itp, kc0)
            emit_scores(itp, kc0 + 1)
            emit_exp(itp, kc0)
            emit_exp(itp, kc0 + 1)
            if kc0 == 0 and itp >= 1:
                emit_tail(itp - 1)
            if NITP == 32 and NGC == 32 and itp % 2 == 1:
                if itp == 23 and kc0 == 0:
                    thr_toks[0] = emit_thr(0, nW1, D * F, sb["thr1"], sb["thrn1"],
                                           0, t_sevacD[itp - 1])
                if itp == NITP - 1 and kc0 == 8:
                    thr_toks[1] = emit_thr(nW1, nW1 + nW2, F * D, sb["thr2"],
                                           sb["thrn2"], 40, t_sevacD[itp - 1])
                if itp == NITP - 1 and kc0 == 10:
                    t_qz1[0] = quantize(0, 1, early=True)
            if kc0 >= 2:
                emit_pv(itp, kc0 - 2)
                emit_pv(itp, kc0 - 1)
            units = weave_units.get(itp, [])
            nsplit = (len(units) + 1) // 2
            if kc0 == KC - 6 and units:
                pe_ = None
                for kind_, dc_, h_ in units[:nsplit]:
                    pe_ = emit_weave_half(itp, kind_, dc_, h_, pe_)
                t_weave_itp[itp] = pe_
            if kc0 == KC - 4 and units:
                pe_ = t_weave_itp.get(itp)
                for kind_, dc_, h_ in units[nsplit:]:
                    pe_ = emit_weave_half(itp, kind_, dc_, h_, pe_)
                t_weave_itp[itp] = pe_
    emit_tail(NITP - 1)

    NDROW = H * NQS
    for i in range(max(0, NITP - 2), NITP):
        for hh in range(2):
            b.wait("act", t_dD[(i, hh)])
    for tk in thr_toks:
        if tk is not None:
            b.wait("act", tk[1])
    t_ln_ = b.op("act", lambda a: a.activation(out=arena1[0:NDROW, 1024:1024 + 512],
                                               in_=Darena[0:NDROW, 0:512], func=AF.Ln))
    b.wait("act", t_ln_)
    t_rden = b.op("act", lambda a: a.activation(out=rdenA[64:64 + NDROW, 0:512],
                                                in_=arena1[0:NDROW, 1024:1024 + 512], func=AF.Exp, scale=-1.0))

    if thr_toks[0] is None:
        thr_toks[0] = emit_thr(0, nW1, D * F, sb["thr1"], sb["thrn1"], 0,
                               t_sevacD[NITP - 1], wait_rden=True)
    if thr_toks[1] is None:
        thr_toks[1] = emit_thr(nW1, nW1 + nW2, F * D, sb["thr2"], sb["thrn2"], 40,
                               t_sevacD[NITP - 1], wait_rden=True)
    t_thr1, t_thr2 = thr_toks

    last_sc = t_sc[NITP * KC - 1]
    last_exp = t_exp[NITP * KC - 1]
    last_pv = t_pv[NITP * KC - 1]

    t_xc = {}
    t_rd = {}
    t_bc = {}
    t_u = {}
    t_t2p = {}
    t_t3 = {}
    t_out = {}
    NQS2 = T // 512

    def emit_combine(dc):
        par = dc % 2
        scr = scrA if par == 0 else scrB
        b.wait("sync", t_qz2[NQ - 1][FCQ - 1])
        if dc >= 2:
            b.wait("sync", t_t3[dc - 2])
        t_xc[dc] = b.dma("sync", lambda s, dc=dc, scr=scr: s.dma_start(
            out=scr[:, 0:T], in_=dr["xT"][dc * 128:(dc + 1) * 128, :]))
        prow = par * 32
        b.wait("gpd", t_rden)
        if dc >= 2:
            b.wait("gpd", t_bc[(dc - 2, nsl - 1)])
        t_rd[dc] = b.dma("gpd", lambda g, dc=dc, prow=prow: g.dma_start(
            out=rdenA[prow:prow + 2, 0:NQS2 * 512].rearrange("p (q c) -> p q c", q=NQS2),
            in_=rdenA[64 + 2 * NQS2 * dc:64 + 2 * NQS2 * (dc + 1), 0:512]))
        b.wait("pe", t_f2e[(NQ - 2, DC - 1)])
        if dc >= 1 and (dc - 1) in t_u:
            b.wait("pe", t_u[dc - 1])
        if dc < 2:
            b.wait("pe", t_E2)
        b.wait("pe", t_rd[dc])
        for s in range(nsl):
            t_bc[(dc, s)] = b.op("pe", lambda t, s=s, prow=prow: t.matmul(
                PS[:, 2048 + s * 512:2048 + (s + 1) * 512],
                sb["E2"][prow:prow + 2, :], rdenA[prow:prow + 2, s * 512:(s + 1) * 512],
                start=True, stop=True))
        ub = sb["ub0"] if par == 0 else sb["ub1"]
        for s in range(nsl):
            b.wait("dve", t_bc[(dc, s)])
        if dc >= 2:
            b.wait("dve", t_t2p[dc - 2])
        if dc == 0:
            b.wait("dve", t_thr1[0])
            b.wait("dve", t_thr2[0])
            b.wait("dve", t_dD[(NITP - 1, 0)])
            b.wait("dve", t_dD[(NITP - 1, 1)])
        t_u[dc] = b.op("dve", lambda v, dc=dc, ub=ub: v.tensor_tensor(
            out=ub[:, 0:T], in0=PS[:, 2048:2048 + T], in1=sraw[:, dc * T:(dc + 1) * T],
            op=ALU.mult))
        b.wait("dve", t_silu[dc])
        b.wait("dve", t_u[dc])
        if dc == 0:
            b.wait("dve", t_f1[(NQ - 1, FCQ - 1, TS - 1)])
        if dc >= 1:
            b.wait("dve", t_t3.get(dc - 1))
        t_t2p[dc] = b.op("dve", lambda v, dc=dc, ub=ub: v.tensor_tensor(
            out=sb["cbuf"][:, 0:T], in0=wslot(2, dc), in1=ub[:, 0:T], op=ALU.mult))
        b.wait("dve", t_xc[dc])
        if dc >= 1:
            b.wait("dve", t_out.get(dc - 1))
        b.wait("dve", t_t2p[dc])
        t_t3[dc] = b.op("dve", lambda v, dc=dc, scr=scr: v.tensor_tensor(
            out=sb["outst"][:, 0:T], in0=sb["cbuf"][:, 0:T], in1=scr[:, 0:T], op=ALU.add))
        b.wait("gpd", t_t3[dc])
        t_out[dc] = b.dma("gpd", lambda g, dc=dc: g.dma_start(
            out=dr["outT"][dc * 128:(dc + 1) * 128, :], in_=sb["outst"][:, 0:T]))

    for qi in range(NQ):
        if qi == 0:
            if 0 not in t_qz1:
                t_qz1[0] = quantize(0, 1)
            t_qz2[0] = quantize(0, 2)
        for fc in range(FCQ):
            base = (fc % 2) * 2048
            for dinc in range(DC):
                b.wait("pe", t_qz1[qi][dinc])
            if qi == 0 and fc < 2:
                b.wait("pe", last_exp)
                b.wait("pe", t_sevacD[NITP - 1])
            if fc >= 2:
                b.wait("pe", t_f1e[(qi, fc - 2)])
            elif qi > 0:
                b.wait("pe", t_f2e[(qi - 1, DC - 2 + (fc % 2))])
            for ts in range(TS):
                for dinc in range(DC):
                    emit = b.op if dinc == DC - 1 else b.opn
                    tok_ = emit("pe", lambda t, fc=fc, ts=ts, dinc=dinc, base=base: t.matmul(
                        PS[:, base + ts * 512:base + (ts + 1) * 512],
                        sb["w1q"][:, dinc * cfg.FQ + fc * 128:dinc * cfg.FQ + (fc + 1) * 128],
                        xn[:, dinc * T + ts * 512:dinc * T + (ts + 1) * 512],
                        start=(dinc == 0), stop=(dinc == DC - 1), skip_group_check=True))
                    if tok_ is not None:
                        t_f1[(qi, fc, ts)] = tok_
            for ts in range(TS):
                b.wait("act", t_f1[(qi, fc, ts)])
            if qi == 0 and fc == 0:
                b.wait("act", t_b1r)
            t_f1e[(qi, fc)] = b.op("act", lambda a, qi=qi, fc=fc, base=base: a.activation(
                out=sb["hq"][:, fc * T:(fc + 1) * T], in_=PS[:, base:base + T],
                func=AF.Relu, bias=sb["b1r"][:, qi * FCQ + fc:qi * FCQ + fc + 1]))
        if qi + 1 < NQ:
            t_qz1[qi + 1] = quantize(qi + 1, 1)
        for dc in range(DC):
            base = (dc % 2) * 2048 if qi < NQ - 1 else 0
            for fc in range(FCQ):
                b.wait("pe", t_qz2[qi][fc])
                b.wait("pe", t_f1e[(qi, fc)])
            if qi < NQ - 1:
                if dc >= 2:
                    b.wait("pe", t_f2e[(qi, dc - 2)])
                else:
                    b.wait("pe", t_f1e[(qi, FCQ - 2 + dc)])
            else:
                if dc >= 1:
                    b.wait("pe", t_f2e[(qi, dc - 1)])
                else:
                    b.wait("pe", t_f1e[(qi, FCQ - 2)])
                    b.wait("pe", t_f1e[(qi, FCQ - 1)])
            for ts in range(TS):
                for fc in range(FCQ):
                    emit = b.op if fc == FCQ - 1 else b.opn
                    tok_ = emit("pe", lambda t, dc=dc, ts=ts, fc=fc, base=base: t.matmul(
                        PS[:, base + ts * 512:base + (ts + 1) * 512],
                        sb["w2q"][:, fc * D + dc * 128:fc * D + (dc + 1) * 128],
                        sb["hq"][:, fc * T + ts * 512:fc * T + (ts + 1) * 512],
                        start=(fc == 0), stop=(fc == FCQ - 1), skip_group_check=True))
                    if tok_ is not None:
                        t_f2[(qi, dc, ts)] = tok_
            if qi == 0:
                for ts in range(TS):
                    b.wait("dve", t_f2[(qi, dc, ts)])
                b.wait("dve", t_b2r)
                te = b.op("dve", lambda v, dc=dc, base=base: v.tensor_single_scalar(
                    out=wslot(0, dc), in_=PS[:, base:base + T],
                    scalar=sb["b2r"][:, dc:dc + 1], op=ALU.add))
                t_f2e[(qi, dc)] = te
            else:
                for ts in range(TS):
                    b.wait("dve", t_f2[(qi, dc, ts)])
                b.wait("dve", t_f2e[(qi - 1, dc)])
                if dc >= 1:
                    b.wait("dve", t_f2e[(qi, dc - 1)])
                if qi < NQ - 1:
                    te = b.op("dve", lambda v, qi=qi, dc=dc, base=base: v.tensor_tensor(
                        out=wslot(qi, dc), in0=wslot(qi - 1, dc), in1=PS[:, base:base + T], op=ALU.add))
                    t_f2e[(qi, dc)] = te
                else:
                    if dc >= 1 and (dc - 1) in t_t2p:
                        b.wait("dve", t_t2p[dc - 1])
                    te = b.op("dve", lambda v, qi=qi, dc=dc, base=base: v.tensor_tensor(
                        out=wslot(3, dc), in0=wslot(2, dc), in1=PS[:, base:base + T], op=ALU.add))
                    t_f2e[(qi, dc)] = te
                    b.wait("act", te)
                    b.wait("act", t_f2e[(qi, dc)])
                    t_silu[dc] = b.op("act", lambda a, dc=dc: a.activation(
                        out=wslot(2, dc), in_=wslot(3, dc), func=AF.Silu))
                    emit_combine(dc)

        if qi + 1 < NQ:
            t_qz2[qi + 1] = quantize(qi + 1, 2)

    for dcw in range(max(0, DC - 4), DC):
        b.wait("gpd", t_out[dcw])



def _prep_inputs(cfg, x, Wq, Wk, Wv, ipw, W1, b1, W2, b2):
    import ml_dtypes
    T, D, F = cfg.T, cfg.D, cfg.F
    perm = perm_for(T)
    wqT = np.ascontiguousarray(np.asarray(Wq, dtype=np.float32).T)
    wkT = np.ascontiguousarray(np.asarray(Wk, dtype=np.float32).T)
    wvT = np.ascontiguousarray(np.asarray(Wv, dtype=np.float32).T)
    w1T = np.ascontiguousarray(np.asarray(W1, dtype=np.float32).T)
    w2T = np.ascontiguousarray(np.asarray(W2, dtype=np.float32).T)
    b1r = np.ascontiguousarray(np.asarray(b1, dtype=np.float32).reshape(F // 128, 128).T)
    b2r = np.ascontiguousarray(np.asarray(b2, dtype=np.float32).reshape(D // 128, 128).T)
    ipw_ = np.asarray(ipw, dtype=np.float32).reshape(1, 1)
    E2 = np.zeros((2, 128), dtype=ml_dtypes.bfloat16)
    E2[0, 0:64] = 1
    E2[1, 64:128] = 1
    x = np.asarray(x, dtype=np.float32)
    in_maps = []
    for c in range(x.shape[0]):
        xT = np.ascontiguousarray(x[c][perm].T)
        in_maps.append({"xT": xT, "wqT": wqT, "wkT": wkT, "wvT": wvT,
                       "w1T": w1T, "w2T": w2T, "b1r": b1r, "b2r": b2r,
                        "ipw": ipw_, "E2": E2})
    return in_maps, perm


def kernel(L0_attn_x, L0_attn_Wq, L0_attn_Wk, L0_attn_Wv, L0_attn_inter_palace_weight,
           L0_ffn_W1, L0_ffn_b1, L0_ffn_W2, L0_ffn_b2):
    cfg = Cfg(T=2048, D=1024, F=4096)
    in_maps, perm = _prep_inputs(cfg, L0_attn_x, L0_attn_Wq, L0_attn_Wk, L0_attn_Wv,
                                 L0_attn_inter_palace_weight, L0_ffn_W1, L0_ffn_b1,
                                 L0_ffn_W2, L0_ffn_b2)
    nc = build_nc(cfg)
    B = len(in_maps)
    res = run_bass_kernel_spmd(nc, in_maps, core_ids=list(range(B)))
    T, D = cfg.T, cfg.D
    out = np.empty((B, T, D), dtype=np.float32)
    for c in range(B):
        out[c][perm] = np.asarray(res.results[c]["outT"]).T
    return out
